# version 26
# baseline (speedup 1.0000x reference)
import sys
sys.path.insert(0, "/opt/trn_rl_repo")
import numpy as np
import ml_dtypes
from contextlib import ExitStack
import concourse.bass as bass
import concourse.mybir as mybir
from concourse.bass_utils import run_bass_kernel_spmd

F32 = mybir.dt.float32
BF16 = mybir.dt.bfloat16
AF = mybir.ActivationFunctionType
ALU = mybir.AluOpType
AX = mybir.AxisListType
bfloat16 = ml_dtypes.bfloat16

D = 1024
NCH = 8
NH = 16
NPAIR = 8
E = 8
W_FF = 8192
NJ = 64
EPS = 1e-5

ENGINES = ("sync", "tensor", "vector", "scalar", "gpsimd")


class Sched:
    def __init__(self):
        from collections import defaultdict
        self.ops = {e: [] for e in ENGINES}
        self.cnt = defaultdict(int)
        self.waited = {e: defaultdict(int) for e in ENGINES}
        self.sems = set(("pe", "act", "dve", "gp"))
        self.free_tok = {}

    def emit(self, eng, fn, inc=None, amt=1):
        tok = None
        if inc is not None:
            self.sems.add(inc)
            self.cnt[inc] += amt
            tok = self.cnt[inc]
        self.ops[eng].append((fn, inc, amt))
        return tok

    def wait(self, eng, sem, tok):
        if tok is None or tok <= 0:
            return
        if self.waited[eng][sem] >= tok:
            return
        self.waited[eng][sem] = tok
        self.ops[eng].append((None, (sem, tok), 0))

    def bwait(self, eng, key):
        if key in self.free_tok:
            s, t = self.free_tok[key]
            self.wait(eng, s, t)

    def bfree(self, key, sem, tok):
        self.free_tok[key] = (sem, tok)

    def barrier(self, sems=None):
        for eng in ENGINES:
            for s in sorted(self.sems if sems is None else sems):
                self.wait(eng, s, self.cnt[s])

    def play(self, nc, semh):
        assert set(semh) >= self.sems, self.sems - set(semh)
        with nc.Block() as block:
            def body(eng_name):
                def run(engobj):
                    for fn, inc, amt in self.ops[eng_name]:
                        if fn is None:
                            s, tok = inc
                            engobj.wait_ge(semh[s], tok)
                        else:
                            instr = fn(engobj)
                            if inc is not None:
                                instr.then_inc(semh[inc], amt)
                return run
            block.sync(body("sync"))
            block.tensor(body("tensor"))
            block.vector(body("vector"))
            block.scalar(body("scalar"))
            block.gpsimd(body("gpsimd"))


def build(S=4096):
    NTB = S // 512
    NKT = S // 128
    NQB = S // 512
    NSLAB = S // 512

    nc = bass.Bass("TRN2", target_bir_lowering=False)
    sch = Sched()
    emit, wait, bwait, bfree = sch.emit, sch.wait, sch.bwait, sch.bfree

    xT = nc.dram_tensor("xT", [D, S], F32, kind="ExternalInput").ap()
    wq = nc.dram_tensor("wq", [D, D], F32, kind="ExternalInput").ap()
    wk = nc.dram_tensor("wk", [D, D], F32, kind="ExternalInput").ap()
    wv = nc.dram_tensor("wv", [D, D], F32, kind="ExternalInput").ap()
    wo = nc.dram_tensor("wo", [D, D], F32, kind="ExternalInput").ap()
    bo = nc.dram_tensor("bo", [128, NCH], F32, kind="ExternalInput").ap()
    rwT = nc.dram_tensor("rwT", [D, E], F32, kind="ExternalInput").ap()
    w1 = nc.dram_tensor("w1", [D, W_FF], F32, kind="ExternalInput").ap()
    w2 = nc.dram_tensor("w2", [W_FF, D], F32, kind="ExternalInput").ap()
    identf = nc.dram_tensor("identf", [128, 128], F32, kind="ExternalInput").ap()
    masktri = nc.dram_tensor("masktri", [128, 128], BF16, kind="ExternalInput").ap()
    outT = nc.dram_tensor("outT", [D, S], F32, kind="ExternalOutput").ap()

    sc_wqb = nc.dram_tensor("sc_wqb", [NCH, 128, D], BF16, kind="Internal").ap()
    sc_wkb = nc.dram_tensor("sc_wkb", [NCH, 128, D], BF16, kind="Internal").ap()
    sc_wvb = nc.dram_tensor("sc_wvb", [NCH, 128, D], BF16, kind="Internal").ap()
    sc_wob = nc.dram_tensor("sc_wob", [NCH, 128, D], BF16, kind="Internal").ap()
    sc_w1b = nc.dram_tensor("sc_w1b", [NCH, 128, W_FF], BF16, kind="Internal").ap()
    sc_w2b = nc.dram_tensor("sc_w2b", [NCH, 128, W_FF], BF16, kind="Internal").ap()
    sc_attn = nc.dram_tensor("sc_attn", [NCH, 128, S], BF16, kind="Internal").ap()
    sc_z = nc.dram_tensor("sc_z", [NPAIR, 2, S], F32, kind="Internal").ap()
    sc_z2 = nc.dram_tensor("sc_z2", [NPAIR, 2, S], BF16, kind="Internal").ap()
    sc_cw = nc.dram_tensor("sc_cw", [8, 512], BF16, kind="Internal").ap()
    sc_x1 = nc.dram_tensor("sc_x1", [NCH, 128, S], F32, kind="Internal").ap()

    ctxO = ExitStack()
    semh = {}

    def gate(eng, qsems, note=""):
        for q in qsems:
            wait(eng, q, sch.cnt[q])
        return emit(eng, lambda e: e.nop(), inc="gp")

    def sbO(name, shape, dt):
        return ctxO.enter_context(nc.sbuf_tensor(name, shape, dt))

    idf = sbO("idf", [128, 128], F32)
    mtri = sbO("mtri", [128, 128], BF16)
    ones_b = sbO("ones_b", [128, 1], BF16)
    ones_c = sbO("ones_c", [128, 1], F32)
    orow_f = sbO("orow_f", [1, 128], F32)
    orow_b = sbO("orow_b", [1, 128], BF16)
    orow_b64 = sbO("orow_b64", [65, 128], BF16)
    bo_sb = sbO("bo_sb", [128, NCH], F32)
    rwTb = sbO("rwTb", [128, E * NCH], BF16)
    m_row = sbO("m_row", [1, 512], F32)
    v_row = sbO("v_row", [1, 512], F32)
    r_row = sbO("r_row", [1, 512], F32)
    t_row = sbO("t_row", [1, 512], F32)

    def dma(eng, q, dst, src):
        return emit(eng, lambda e, d=dst, s=src: e.dma_start(d, s), inc=q, amt=16)

    dma("sync", "dc", idf[:, :], identf[:, :])
    dma("sync", "dc", mtri[:, :], masktri[:, :])
    dma("sync", "dc", bo_sb[:, :], bo[:, :])
    emit("vector", lambda e: e.memset(ones_b[:, :], 1.0), inc="dve")
    emit("vector", lambda e: e.memset(ones_c[:, :], 1.0), inc="dve")
    emit("vector", lambda e: e.memset(orow_f[:, :], 1.0), inc="dve")
    emit("vector", lambda e: e.memset(orow_b[:, :], 1.0), inc="dve")
    tok_ones = emit("vector", lambda e: e.memset(orow_b64[64:65, :], 1.0), inc="dve")

    eps_sb = sbO("eps_sb", [1, 1], F32)
    tok_eps = emit("vector", lambda e: e.memset(eps_sb[:, :], EPS), inc="dve")
    nc.const_aps.register(F32, EPS, eps_sb[0:1, 0:1])
    rwf = sbO("rwf", [128, E * NCH], F32)
    tok_dc = dma("sync", "dc", rwf[:, :].rearrange("p (c e) -> p c e", c=NCH),
                 rwT.rearrange("(c p) e -> p c e", p=128))
    wait("vector", "dc", tok_dc)
    tok_rwb = emit("vector", lambda e: e.tensor_copy(rwTb[:, :], rwf[:, :]), inc="dve")
    tok_dc_all = sch.cnt["dc"]

    def ln_block(xs_img, out_img_fn, st_ps, st_key, bc_ps, bc_key, xb2, sq2, t1b):
        wait("tensor", "dve", tok_ones)
        t_mm = None
        for c in range(NCH):
            par = c % 2
            xbb, sqb = xb2[par], sq2[par]
            bwait("vector", f"xb:{xbb.name}")
            t_xb = emit("vector", lambda e, dst=xbb[:, :], src=xs_img[:, c, :]:
                        e.tensor_copy(dst, src), inc="dve")
            wait("vector", "dve", t_xb)
            bwait("vector", f"sq:{sqb.name}")
            t_sq = emit("vector", lambda e, dst=sqb[:, :], s=xbb[:, :]:
                        e.tensor_mul(dst, s, s), inc="dve")
            wait("tensor", "dve", t_sq)
            if c == 0:
                bwait("tensor", st_key)
            emit("tensor", lambda e, dst=st_ps[0:1, 0:512], s=xbb[:, :], c=c:
                 e.matmul(dst, ones_b[:, :], s, start=(c == 0), stop=(c == NCH - 1)))
            t_mm = emit("tensor", lambda e, dst=st_ps[0:1, 512:1024], s=sqb[:, :], c=c:
                        e.matmul(dst, ones_b[:, :], s, start=(c == 0), stop=(c == NCH - 1)),
                        inc="pe")
            bfree(f"xb:{xbb.name}", "pe", t_mm)
            bfree(f"sq:{sqb.name}", "pe", t_mm)
        wait("vector", "pe", t_mm)
        bwait("vector", "lnrows")
        t = emit("vector", lambda e, s=st_ps[0:1, 0:512]: e.tensor_scalar_mul(m_row[:, :], s, 1.0 / D), inc="dve")
        wait("vector", "dve", t)
        t = emit("vector", lambda e, s=st_ps[0:1, 512:1024]: e.tensor_scalar_mul(v_row[:, :], s, 1.0 / D), inc="dve")
        wait("vector", "dve", t)
        t = emit("vector", lambda e: e.tensor_mul(t_row[:, :], m_row[:, :], m_row[:, :]), inc="dve")
        wait("vector", "dve", t)
        t_var = emit("vector", lambda e: e.tensor_sub(v_row[:, :], v_row[:, :], t_row[:, :]), inc="dve")
        bfree(st_key, "dve", t_var)
        wait("scalar", "dve", t_var)
        wait("scalar", "dve", tok_eps)
        t = emit("scalar", lambda e: e.activation(t_row[:, :], v_row[:, :], AF.Ln, bias=EPS), inc="act")
        wait("scalar", "act", t)
        t_r = emit("scalar", lambda e: e.activation(r_row[:, :], t_row[:, :], AF.Exp, scale=-0.5), inc="act")
        wait("tensor", "act", t_r)
        bwait("tensor", bc_key)
        emit("tensor", lambda e, dst=bc_ps[:, 0:512]: e.matmul(dst, orow_f[:, :], m_row[:, :], start=True, stop=True))
        t_bc = emit("tensor", lambda e, dst=bc_ps[:, 512:1024]: e.matmul(dst, orow_f[:, :], r_row[:, :], start=True, stop=True), inc="pe")
        bfree("lnrows", "pe", t_bc)
        wait("vector", "pe", t_bc)
        t_ap = None
        for c in range(NCH):
            t = emit("vector", lambda e, dst=t1b[:, :], src=xs_img[:, c, :], mb=bc_ps[:, 0:512]:
                     e.tensor_sub(dst, src, mb), inc="dve")
            wait("vector", "dve", t)
            t_ap = emit("vector", lambda e, dst=out_img_fn(c), src=t1b[:, :], rb=bc_ps[:, 512:1024]:
                        e.tensor_mul(dst, src, rb), inc="dve")
            wait("vector", "dve", t_ap)
        bfree(bc_key, "dve", t_ap)
        return t_ap

    ctxA = ExitStack()

    def sbA(name, shape, dt):
        return ctxA.enter_context(nc.sbuf_tensor(name, shape, dt))

    cvf = [sbA(f"cvf{i}", [128, NCH * 128], F32) for i in range(2)]
    cvb = [sbA("cvb0", [128, NCH * 128], BF16)] * 2
    xnT = sbA("xnT", [128, NCH * S], BF16)
    Vst = sbA("Vst", [128, NKT * NH * 65], BF16)


    cv_par = [0]
    cv_eng = ["gpsimd"]

    def conv_chunk(srcap, dstap):
        ce = cv_eng[0]
        csem = "act" if ce == "scalar" else "gp"
        par = cv_par[0] % 2
        cv_par[0] += 1
        fb = cvf[par][:, :].rearrange("p (c j) -> p c j", c=NCH)
        bb = cvb[par][:, :].rearrange("p (c j) -> p c j", c=NCH)
        bwait("gpsimd", f"cvf{par}")
        t_ld = dma("gpsimd", f"dcl{par}", fb, srcap)
        wait(ce, f"dcl{par}", t_ld)
        bwait(ce, "cvb")
        t_cv = emit(ce, lambda e, bb=bb, fb=fb:
                    (e.copy(bb, fb) if ce == "scalar" else e.tensor_copy(bb, fb)), inc=csem)
        bfree(f"cvf{par}", csem, t_cv)
        wait("gpsimd", csem, t_cv)
        t_st = dma("gpsimd", f"dcs{par}", dstap, bb)
        bfree("cvb", f"dcs{par}", t_st)
        return t_st

    def conv_square(w_ap, sc_ap):
        for i in range(8):
            a = i * 128
            conv_chunk(w_ap[:, a:a + 128].rearrange("(c p) j -> p c j", p=128),
                       sc_ap[:, :, a:a + 128].rearrange("c p j -> p c j"))
        return gate("gpsimd", ["dcs0", "dcs1"])

    def conv_w1_chunk(i):
        a = i * 128
        J, u = i // 8, i % 8
        dst = sc_w1b[J, :, u * 1024:(u + 1) * 1024].rearrange("p (c j) -> p c j", c=NCH)
        return conv_chunk(w1[:, a:a + 128].rearrange("(c p) j -> p c j", p=128), dst)

    def conv_w2_chunk(i):
        ce = cv_eng[0]
        csem = "act" if ce == "scalar" else "gp"
        C, mi = i // 8, i % 8
        m0 = mi * 128
        par = cv_par[0] % 2
        cv_par[0] += 1
        fb = cvf[par][:, :].rearrange("p (js m) -> p js m", js=8)
        bb = cvb[par][:, :].rearrange("p (js m) -> p js m", js=8)
        bwait("gpsimd", f"cvf{par}")
        t_ld = dma("gpsimd", f"dcl{par}", fb,
                   w2[C * 1024:(C + 1) * 1024, m0:m0 + 128].rearrange("(js p) m -> p js m", p=128))
        wait(ce, f"dcl{par}", t_ld)
        bwait(ce, "cvb")
        t_cv = emit(ce, lambda e, bb=bb, fb=fb:
                    (e.copy(bb, fb) if ce == "scalar" else e.tensor_copy(bb, fb)), inc=csem)
        bfree(f"cvf{par}", csem, t_cv)
        wait("gpsimd", csem, t_cv)
        MT = mi
        dst = sc_w2b[MT, :, :].rearrange("p (jt m) -> p jt m", jt=NJ)[:, C * 8:(C + 1) * 8, 0:128]
        t_st = dma("gpsimd", f"dcs{par}", dst, bb[:, :, 0:128])
        bfree("cvb", f"dcs{par}", t_st)
        return t_st

    tok_wv_cv = conv_square(wv, sc_wvb)
    tok_wq_cv = conv_square(wq, sc_wqb)
    tok_wk_cv = conv_square(wk, sc_wkb)

    ctxP01 = ExitStack()
    stat_ps = ctxP01.enter_context(nc.psum_tensor("stat_ps", [1, 1024], F32))
    bc_ps = ctxP01.enter_context(nc.psum_tensor("bc_ps", [128, 1024], F32))
    vp_ps = [ctxP01.enter_context(nc.psum_tensor(f"vp{i}", [128, 1024], F32)) for i in range(2)]

    ctxA0 = ExitStack()
    def sbA0(name, shape, dt):
        return ctxA0.enter_context(nc.sbuf_tensor(name, shape, dt))
    x0s = [sbA0(f"x0s{i}", [128, NCH * 512], F32) for i in range(2)]
    sqtA = [sbA0(f"sqtA{i}", [128, 512], F32) for i in range(2)]
    t1sA = sbA0("t1sA", [128, 512], F32)
    wvb = sbA0("wvb", [128, NCH * D], BF16)

    bg_conv = ([("wo", i) for i in range(8)] + [("w1", i) for i in range(64)]
               + [("w2", i) for i in range(64)])
    bg_done = {}

    def run_bg_conv(n):
        for _ in range(n):
            if not bg_conv:
                return
            kind, i = bg_conv.pop(0)
            if kind == "wo":
                a = i * 128
                conv_chunk(wo[:, a:a + 128].rearrange("(c p) j -> p c j", p=128),
                           sc_wob[:, :, a:a + 128].rearrange("c p j -> p c j"))
            elif kind == "w1":
                conv_w1_chunk(i)
            else:
                conv_w2_chunk(i)
            if not bg_conv or bg_conv[0][0] != kind:
                bg_done[kind] = gate("gpsimd", ["dcs0", "dcs1"])

    t_vones = emit("vector", lambda e:
                   e.memset(Vst[:, :].rearrange("p (k h w) -> p k h w", k=NKT, h=NH)[:, :, :, 64:65], 1.0),
                   inc="dve")
    wait("sync", "gp", tok_wv_cv)
    t_wvb = dma("sync", "dwv", wvb[:, :].rearrange("p (c i) -> p c i", c=NCH),
                sc_wvb.rearrange("c p i -> p c i"))
    wait("tensor", "dwv", t_wvb)
    wait("tensor", "dve", t_vones)

    v_done = None
    lnst = {}

    def p0_dma(t):
        par = t % 2
        bwait("sync", f"x0s{par}")
        lnst[("x", t)] = dma("sync", f"dx0_{par}",
                             x0s[par][:, :].rearrange("p (c t) -> p c t", c=NCH),
                             xT[:, t * 512:(t + 1) * 512].rearrange("(c p) t -> p c t", p=128))

    def p0_stats(t):
        par = t % 2
        xs = x0s[par][:, :].rearrange("p (c t) -> p c t", c=NCH)
        wait("vector", f"dx0_{par}", lnst[("x", t)])
        wait("tensor", "dve", tok_ones)
        t_mm = None
        bwait("tensor", "stat_ps")
        for c in range(NCH):
            pp = c % 2
            bwait("vector", f"sq:{sqtA[pp].name}")
            t_sq = emit("vector", lambda e, dst=sqtA[pp][:, :], src=xs[:, c, :]:
                        e.tensor_mul(dst, src, src), inc="dve")
            wait("vector", "dve", t_sq)
            wait("tensor", f"dx0_{par}", lnst[("x", t)])
            emit("tensor", lambda e, src=xs[:, c, :], c=c:
                 e.matmul(stat_ps[0:1, 0:512], ones_c[:, :], src, start=(c == 0), stop=(c == NCH - 1)))
            wait("tensor", "dve", t_sq)
            t_mm = emit("tensor", lambda e, src=sqtA[pp][:, :], c=c:
                        e.matmul(stat_ps[0:1, 512:1024], ones_c[:, :], src, start=(c == 0), stop=(c == NCH - 1)),
                        inc="pe")
            bfree(f"sq:{sqtA[pp].name}", "pe", t_mm)
        lnst[("mm", t)] = t_mm

    def p0_fin(t):
        wait("vector", "pe", lnst[("mm", t)])
        bwait("vector", "lnrows")
        tt = emit("vector", lambda e: e.tensor_scalar_mul(m_row[:, :], stat_ps[0:1, 0:512], 1.0 / D), inc="dve")
        wait("vector", "dve", tt)
        tt = emit("vector", lambda e: e.tensor_scalar_mul(v_row[:, :], stat_ps[0:1, 512:1024], 1.0 / D), inc="dve")
        wait("vector", "dve", tt)
        tt = emit("vector", lambda e: e.tensor_mul(t_row[:, :], m_row[:, :], m_row[:, :]), inc="dve")
        wait("vector", "dve", tt)
        t_var = emit("vector", lambda e: e.tensor_sub(v_row[:, :], v_row[:, :], t_row[:, :]), inc="dve")
        bfree("stat_ps", "dve", t_var)
        wait("scalar", "dve", t_var)
        wait("scalar", "dve", tok_eps)
        tt = emit("scalar", lambda e: e.activation(t_row[:, :], v_row[:, :], AF.Ln, bias=EPS), inc="act")
        wait("scalar", "act", tt)
        lnst[("r", t)] = emit("scalar", lambda e: e.activation(r_row[:, :], t_row[:, :], AF.Exp, scale=-0.5), inc="act")

    def p0_bcast(t):
        wait("tensor", "act", lnst[("r", t)])
        bwait("tensor", "bc_ps")
        emit("tensor", lambda e: e.matmul(bc_ps[:, 0:512], orow_f[:, :], m_row[:, :], start=True, stop=True))
        t_bc = emit("tensor", lambda e: e.matmul(bc_ps[:, 512:1024], orow_f[:, :], r_row[:, :], start=True, stop=True), inc="pe")
        bfree("lnrows", "pe", t_bc)
        lnst[("bc", t)] = t_bc

    def p0_apply(t):
        par = t % 2
        xs = x0s[par][:, :].rearrange("p (c t) -> p c t", c=NCH)
        wait("vector", "pe", lnst[("bc", t)])
        t_ap = None
        for c in range(NCH):
            tt = emit("vector", lambda e, dst=t1sA[:, :], src=xs[:, c, :]:
                      e.tensor_sub(dst, src, bc_ps[:, 0:512]), inc="dve")
            wait("vector", "dve", tt)
            t_ap = emit("vector", lambda e, dst=xnT[:, c * S + t * 512: c * S + (t + 1) * 512], src=t1sA[:, :]:
                        e.tensor_mul(dst, src, bc_ps[:, 512:1024]), inc="dve")
            wait("vector", "dve", t_ap)
        bfree("bc_ps", "dve", t_ap)
        bfree(f"x0s{par}", "dve", t_ap)
        lnst[("ap", t)] = t_ap

    def p0_v(t):
        nonlocal v_done
        wait("tensor", "dve", lnst[("ap", t)])
        for kt in range(4 * t, 4 * t + 4):
            vpar = kt % 2
            bwait("tensor", f"vp{vpar}")
            t_v = None
            for c in range(NCH):
                for h in range(2):
                    t_v = emit("tensor", lambda e, dst=vp_ps[vpar][:, h * 512:(h + 1) * 512],
                               l=xnT[:, c * S + kt * 128: c * S + kt * 128 + 128],
                               r=wvb[:, c * D + h * 512: c * D + (h + 1) * 512], c=c:
                               e.matmul(dst, l, r, start=(c == 0), stop=(c == NCH - 1)),
                               inc=("pe" if (c == NCH - 1 and h == 1) else None))
            wait("vector", "pe", t_v)
            dst = Vst[:, kt * 1040:(kt + 1) * 1040].rearrange("p (h w) -> p h w", h=NH)
            t_ev = emit("vector", lambda e, d=dst[:, :, 0:64],
                        s=vp_ps[vpar][:, :].rearrange("p (h w) -> p h w", h=NH, w=64):
                        e.tensor_copy(d, s), inc="dve")
            bfree(f"vp{vpar}", "dve", t_ev)
            v_done = t_ev

    p0_dma(0)
    p0_stats(0)
    p0_fin(0)
    for t in range(NTB):
        if t + 1 < NTB:
            p0_dma(t + 1)
        p0_bcast(t)
        if t + 1 < NTB:
            p0_stats(t + 1)
        p0_apply(t)
        if t + 1 < NTB:
            p0_fin(t + 1)
        p0_v(t)

    sch.barrier(sems=("pe", "act", "dve"))
    ctxP01.close()
    ctxA0.close()

    qkwq = [sbA(f"qkwq{i}", [128, NCH * 128], BF16) for i in range(2)]
    qkwk = [sbA(f"qkwk{i}", [128, NCH * 128], BF16) for i in range(2)]
    QT = sbA("QT", [128, S], BF16)
    KT = [sbA(f"KT{i}", [128, S], BF16) for i in range(2)]
    ptil = [sbA(f"ptil{i}", [128, 1024], BF16) for i in range(3)]
    SHALF = min(2048, S)
    QBH = SHALF // 512
    stgA = sbA("stgA", [64, SHALF], BF16)
    stgB = sbA("stgB", [64, SHALF], BF16)
    zraw = sbA("zraw", [65, 1024], F32)

    ctxP2 = ExitStack()
    s_ps = [ctxP2.enter_context(nc.psum_tensor(f"s_ps{i}", [128, 1024], F32)) for i in range(2)]
    pvA = ctxP2.enter_context(nc.psum_tensor("pvA", [128, 512], F32))
    pvB = ctxP2.enter_context(nc.psum_tensor("pvB", [128, 512], F32))
    qk_ps = [ctxP2.enter_context(nc.psum_tensor(f"qk_ps{i}", [128, 512], F32)) for i in range(2)]

    sexp_tok = {}
    exp_tok = {}
    pv_tok = {}
    gt = [0]
    ktq_tok = {}
    last_s_tok = {}

    def emit_qkw_load(p):
        pw = p % 2
        wait("scalar", "gp", tok_wq_cv)
        wait("scalar", "gp", tok_wk_cv)
        bwait("scalar", f"qkwq{pw}")
        dma("scalar", f"dqk{pw}", qkwq[pw][:, :].rearrange("p (c i) -> p c i", c=NCH),
            sc_wqb[:, :, p * 128:(p + 1) * 128].rearrange("c p i -> p c i"))
        bwait("scalar", f"qkwk{pw}")
        t_wk = dma("scalar", f"dqk{pw}", qkwk[pw][:, :].rearrange("p (c i) -> p c i", c=NCH),
                   sc_wkb[:, :, p * 128:(p + 1) * 128].rearrange("c p i -> p c i"))
        return t_wk

    qkw_tok = {}

    def emit_proj_group(p, tb, guard_pair):
        pw = p % 2
        wait("tensor", f"dqk{pw}", qkw_tok[p])
        t_evs = []
        for wbuf, bank, dstbuf, do_scale in ((qkwq[pw], 0, QT, True), (qkwk[pw], 1, KT[pw], False)):
            bwait("tensor", f"qk_ps{bank}")
            t_pr = None
            for c in range(NCH):
                t_pr = emit("tensor", lambda e, dst=qk_ps[bank][:, :],
                            l=wbuf[:, c * 128:(c + 1) * 128],
                            r=xnT[:, c * S + tb * 512: c * S + (tb + 1) * 512], c=c:
                            e.matmul(dst, l, r, start=(c == 0), stop=(c == NCH - 1)),
                            inc=("pe" if c == NCH - 1 else None))
            if do_scale and guard_pair is not None:
                wait("vector", "pe", last_s_tok[(guard_pair, tb)])
            if not do_scale:
                bwait("vector", f"KTbuf{pw}")
            wait("vector", "pe", t_pr)
            if do_scale:
                t_ev = emit("vector", lambda e, d=dstbuf[:, tb * 512:(tb + 1) * 512],
                            s=qk_ps[bank][:, :]: e.tensor_scalar_mul(d, s, 0.125), inc="dve")
            else:
                t_ev = emit("vector", lambda e, d=dstbuf[:, tb * 512:(tb + 1) * 512],
                            s=qk_ps[bank][:, :]: e.tensor_copy(d, s), inc="dve")
            bfree(f"qk_ps{bank}", "dve", t_ev)
            t_evs.append(t_ev)
            bfree("qkwq%d" % pw if do_scale else "qkwk%d" % pw, "pe", t_pr)
        ktq_tok[(p, tb)] = t_evs[-1]
        return t_evs

    pv_act_free = [None]

    wait("tensor", "dve", v_done)
    qkw_tok[0] = emit_qkw_load(0)
    for tb in range(NTB):
        emit_proj_group(0, tb, None)

    for p in range(NPAIR):
        pw = p % 2
        if p + 1 < NPAIR:
            qkw_tok[p + 1] = emit_qkw_load(p + 1)

        lazy_q = []
        t_ev = None

        def do_lazy_slot():
            if not lazy_q:
                return
            kind, arg = lazy_q.pop(0)
            emit_proj_group(p + 1, arg, p)

        for qb in range(NQB):
            nk = 4 * qb + 4
            tk0 = gt[0]
            wait("tensor", "dve", ktq_tok[(p, qb)])

            def emit_S(j):
                tj = tk0 + j
                par = tj % 2
                par3 = tj % 3
                r_ = j - 4 * qb
                off = 128 * r_ if r_ >= 0 else 0
                wait("tensor", "act", sexp_tok.get(tj - 2))
                emit("tensor", lambda e, dst=s_ps[par][:, off:512],
                     l=KT[pw][0:64, j * 128:(j + 1) * 128],
                     r=QT[0:64, qb * 512 + off:(qb + 1) * 512]:
                     e.matmul(dst, l, r, start=True, stop=True, tile_position=(0, 0)))
                t_s = emit("tensor", lambda e, dst=s_ps[par][:, 512 + off:1024],
                           l=KT[pw][64:128, j * 128:(j + 1) * 128],
                           r=QT[64:128, qb * 512 + off:(qb + 1) * 512]:
                           e.matmul(dst, l, r, start=True, stop=True, tile_position=(64, 0)),
                           inc="pe")
                wait("scalar", "pe", t_s)
                wait("scalar", "pe", pv_tok.get(tj - 3))
                src2 = s_ps[par][:, :].rearrange("p (h q) -> p h q", h=2)[:, :, off:512]
                dst2 = ptil[par3][:, :].rearrange("p (h q) -> p h q", h=2)[:, :, off:512]
                t_e = emit("scalar", lambda e, dst=dst2, src=src2:
                           e.activation(dst, src, AF.Exp), inc="act")
                sexp_tok[tj] = t_e
                exp_tok[tj] = ("act", t_e)
                if r_ >= 0:
                    wait("vector", "act", t_e)
                    wait("vector", "dc", tok_dc_all)
                    emit("vector", lambda e, a=ptil[par3][:, off:off + 128]:
                         e.tensor_mul(a, a, mtri[:, :]), inc="dve")
                    t_m = emit("vector", lambda e, a=ptil[par3][:, 512 + off:512 + off + 128]:
                               e.tensor_mul(a, a, mtri[:, :]), inc="dve")
                    exp_tok[tj] = ("dve", t_m)
                return t_s

            def emit_PV(j):
                tj = tk0 + j
                par3 = tj % 3
                et = exp_tok[tj]
                wait("tensor", et[0], et[1])
                if j == 0:
                    bwait("tensor", "pvbanks")
                    if pv_act_free[0] is not None:
                        wait("tensor", "act", pv_act_free[0])
                r_ = j - 4 * qb
                off = 128 * r_ if r_ >= 0 else 0
                emit("tensor", lambda e, dst=pvA[0:65, off:512],
                     l=Vst[:, j * 1040 + 65 * (2 * p): j * 1040 + 65 * (2 * p) + 65],
                     r=ptil[par3][:, off:512], j=j:
                     e.matmul(dst, l, r, start=(j == 0), stop=(j == nk - 1)))
                t_pv = emit("tensor", lambda e, dst=pvB[0:65, off:512],
                            l=Vst[:, j * 1040 + 65 * (2 * p + 1): j * 1040 + 65 * (2 * p + 1) + 65],
                            r=ptil[par3][:, 512 + off:1024], j=j:
                            e.matmul(dst, l, r, start=(j == 0), stop=(j == nk - 1)),
                            inc="pe")
                pv_tok[tj] = t_pv
                return t_pv

            t_s_last = emit_S(0)
            t_s_last = emit_S(1)
            for j in range(2, nk):
                t_s_last = emit_S(j)
                emit_PV(j - 2)
            do_lazy_slot()
            emit_PV(nk - 2)
            t_pv_last = emit_PV(nk - 1)
            last_s_tok[(p, qb)] = t_s_last
            gt[0] += nk

            hcol = (qb % QBH) * 512
            wait("vector", "pe", t_pv_last)
            bwait("vector", "stg")
            bwait("vector", "rzbuf")
            emit("vector", lambda e, d=zraw[64:65, 0:512]: e.tensor_copy(d, pvA[64:65, :]), inc="dve")
            emit("vector", lambda e, d=zraw[64:65, 512:1024]: e.tensor_copy(d, pvB[64:65, :]), inc="dve")
            emit("vector", lambda e, d=stgA[0:64, hcol:hcol + 512]:
                 e.tensor_copy(d, pvA[0:64, :]), inc="dve")
            t_u = emit("vector", lambda e, d=stgB[0:64, hcol:hcol + 512]:
                       e.tensor_copy(d, pvB[0:64, :]), inc="dve")
            sch.free_tok["pvbanks"] = ("dve", t_u)
            pv_act_free[0] = None
            wait("sync", "dve", t_u)
            t_dz = dma("sync", "dsz",
                       sc_z[p, :, qb * 512:(qb + 1) * 512],
                       zraw[64:65, :].rearrange("q (h t) -> q h t", h=2))
            bfree("rzbuf", "dsz", t_dz)
            if p + 1 < NPAIR and qb < NQB - 1:
                lazy_q.append(("proj", qb))

            if qb % QBH == QBH - 1 or qb == NQB - 1:
                while lazy_q:
                    do_lazy_slot()
                half = qb // QBH
                wait("sync", "dve", t_u)
                t1 = dma("sync", "dsg", sc_attn[p, 0:64, half * SHALF:(half + 1) * SHALF], stgA[:, :])
                t2 = dma("sync", "dsg", sc_attn[p, 64:128, half * SHALF:(half + 1) * SHALF], stgB[:, :])
                bfree("stg", "dsg", t2)
        if p + 1 < NPAIR:
            emit_proj_group(p + 1, NQB - 1, p)
        bfree(f"KTbuf{pw}", "pe", last_s_tok[(p, NQB - 1)])
        run_bg_conv(17)

    run_bg_conv(200)
    tok_wo_cv = bg_done["wo"]
    tok_w1_cv = bg_done["w1"]
    tok_w2_cv = bg_done["w2"]

    sch.barrier()
    ctxP2.close()
    ctxA.close()
    ctxB = ExitStack()

    def sbB(name, shape, dt):
        return ctxB.enter_context(nc.sbuf_tensor(name, shape, dt))

    wob = sbB("wob", [128, NCH * D], BF16)
    attn_in = [sbB(f"attn_in{i}", [128, NCH * 512], BF16) for i in range(2)]
    attn_s = [sbB(f"attn_s{i}", [128, NCH * 512], BF16) for i in range(2)]
    zrb = [sbB(f"zrb{i}", [128, NCH * 512], BF16) for i in range(2)]
    xTs = [sbB(f"xTs{i}", [128, NCH * 512], F32) for i in range(2)]
    x1st = [sbB(f"x1st{i}", [128, NCH * 512], F32) for i in range(2)]
    rz1 = [sbB(f"rz1_{i}", [1, 16 * 512], BF16) for i in range(2)]
    ZF = (NPAIR * 2 * S) // 128
    zq_f = sbB("zq_f", [128, ZF], F32)
    zq_b = sbB("zq_b", [128, ZF], BF16)

    ctxP3 = ExitStack()
    po = [ctxP3.enter_context(nc.psum_tensor(f"po{i}", [128, 512], F32)) for i in range(8)]

    wait("sync", "gp", tok_wo_cv)
    t_wob = dma("sync", "dwo", wob[:, :].rearrange("p (c i) -> p c i", c=NCH),
                sc_wob.rearrange("c p i -> p c i"))

    p3ld = {}

    t_zl = dma("scalar", "dzq", zq_f[:, :], bass.AP(sc_z.tensor, 0, [[ZF, 128], [1, ZF]]))
    wait("vector", "dzq", t_zl)
    t_zr = emit("vector", lambda e: e.reciprocal(zq_f[:, :], zq_f[:, :]), inc="dve")
    wait("vector", "dve", t_zr)
    t_zc = emit("vector", lambda e: e.tensor_copy(zq_b[:, :], zq_f[:, :]), inc="dve")
    wait("scalar", "dve", t_zc)
    t_zs = dma("scalar", "dzq", bass.AP(sc_z2.tensor, 0, [[ZF, 128], [1, ZF]]), zq_b[:, :])

    def p3_loads(tb):
        par = tb % 2
        bwait("sync", f"attn_in{par}")
        t_a = dma("sync", f"dai{par}", attn_in[par][:, :].rearrange("p (c t) -> p c t", c=NCH),
                  sc_attn[:, :, tb * 512:(tb + 1) * 512].rearrange("c p t -> p c t"))
        bwait("scalar", f"rz1_{par}")
        wait("scalar", "dzq", t_zs)
        t_z = dma("scalar", f"dzr{par}", rz1[par][0:1, :].rearrange("q (k t) -> q k t", k=16),
                  bass.AP(sc_z2.tensor, tb * 512, [[0, 1], [S, 16], [1, 512]]))
        bwait("gpsimd", f"xTs{par}")
        t_xr = dma("gpsimd", f"dxr{par}", xTs[par][:, :].rearrange("p (c t) -> p c t", c=NCH),
                   xT[:, tb * 512:(tb + 1) * 512].rearrange("(c p) t -> p c t", p=128))
        p3ld[tb] = (t_a, t_z, t_xr)

    p3_loads(0)
    for tb in range(NTB):
        par = tb % 2
        if tb + 1 < NTB:
            p3_loads(tb + 1)
        t_a, t_z, t_xr = p3ld[tb]
        wait("tensor", f"dzr{par}", t_z)
        wait("tensor", "dve", tok_ones)
        bwait("vector", f"zrb{par}")
        bc_mm = {}
        bc_ev = {}
        for k in range(16):
            b = k % 2
            cc, hh = k // 2, k % 2
            if k >= 2:
                wait("tensor", "dve", bc_ev[k - 2])
            else:
                bwait("tensor", f"po{b}")
            bc_mm[k] = emit("tensor", lambda e, dst=po[b][0:64, :],
                            r=rz1[par][0:1, k * 512:(k + 1) * 512]:
                            e.matmul(dst, orow_b[0:1, 0:64], r, start=True, stop=True),
                            inc="pe")
            wait("vector", "pe", bc_mm[k])
            bc_ev[k] = emit("vector", lambda e,
                            d=zrb[par][64 * hh:64 * hh + 64, cc * 512:(cc + 1) * 512],
                            s=po[b][0:64, :]: e.tensor_copy(d, s), inc="dve")
        t_zbc = bc_ev[15]
        bfree("po0", "dve", bc_ev[14])
        bfree("po1", "dve", bc_ev[15])
        bfree(f"rz1_{par}", "pe", bc_mm[15])
        wait("gpsimd", f"dai{par}", t_a)
        wait("gpsimd", "dve", t_zbc)
        bwait("gpsimd", f"attn_s{par}")
        t_sc = emit("gpsimd", lambda e, d=attn_s[par][:, :], a=attn_in[par][:, :],
                    z=zrb[par][:, :]: e.tensor_mul(d, a, z), inc="gp")
        bfree(f"attn_in{par}", "gp", t_sc)
        bfree(f"zrb{par}", "gp", t_sc)
        wait("tensor", "gp", t_sc)
        if tb == 0:
            wait("tensor", "dwo", t_wob)
        mt_toks = []
        for mt in range(8):
            bwait("tensor", f"po{mt}")
            t_mm = None
            for c in range(NCH):
                t_mm = emit("tensor", lambda e, dst=po[mt][:, :],
                            l=wob[:, c * D + mt * 128: c * D + mt * 128 + 128],
                            r=attn_s[par][:, c * 512:(c + 1) * 512], c=c:
                            e.matmul(dst, l, r, start=(c == 0), stop=(c == NCH - 1)),
                            inc=("pe" if c == NCH - 1 else None))
            mt_toks.append(t_mm)
        wait("vector", f"dxr{par}", t_xr)
        wait("vector", "dc", tok_dc_all)
        t_x1 = None
        for mt in range(8):
            wait("vector", "pe", mt_toks[mt])
            t_x1 = emit("vector", lambda e, d=x1st[par][:, mt * 512:(mt + 1) * 512],
                        s=po[mt][:, :], b=bo_sb[:, mt:mt + 1],
                        x=xTs[par][:, mt * 512:(mt + 1) * 512]:
                        e.scalar_tensor_tensor(d, s, b, x, ALU.add, ALU.add), inc="dve")
            bfree(f"po{mt}", "dve", t_x1)
        bfree(f"attn_s{par}", "pe", mt_toks[-1])
        bfree(f"xTs{par}", "dve", t_x1)
        wait("gpsimd", "dve", t_x1)
        t_sx1 = dma("gpsimd", f"dsx{par}", sc_x1[:, :, tb * 512:(tb + 1) * 512].rearrange("c p t -> p c t"),
                    x1st[par][:, :].rearrange("p (c t) -> p c t", c=NCH))
        bfree(f"x1st{par}", f"dsx{par}", t_sx1)

    sch.barrier()
    ctxP3.close()
    ctxB.close()
    ctxC = ExitStack()

    def sbC(name, shape, dt):
        return ctxC.enter_context(nc.sbuf_tensor(name, shape, dt))

    x1s = [sbC(f"x1s{i}", [128, NCH * 512], F32) for i in range(2)]
    xn2s = [sbC(f"xn2s{i}", [128, NCH * 512], BF16) for i in range(2)]
    aT = sbC("aT", [128, NJ * 512], BF16)
    w1buf = [sbC(f"w1buf{i}", [128, W_FF], BF16) for i in range(2)]
    w2buf = [sbC(f"w2buf{i}", [128, W_FF // 2], BF16) for i in range(3)]
    cwb = sbC("cwb", [128, E * 512], BF16)
    rbuf = [sbC(f"rbuf{i}", [128, 512], BF16) for i in range(2)]
    tbuf = [sbC(f"tbuf{i}", [128, 512], BF16) for i in range(2)]
    ostg = [sbC(f"ostg{i}", [128, 512], F32) for i in range(2)]

    t1sC = sbC("t1sC", [128, 512], F32)
    prb = sbC("prb", [8, 512], F32)
    pn = sbC("pn", [128, 32], F32)
    tmp8 = sbC("tmp8", [128, 32], F32)
    m1c = sbC("m1c", [128, 1], F32)
    m2c = sbC("m2c", [128, 1], F32)
    zc = sbC("zc", [128, 1], F32)
    cwf = sbC("cwf", [128, 32], F32)
    cwT = sbC("cwT", [8, 512], BF16)

    ctxP4 = ExitStack()
    ST = ctxP4.enter_context(nc.psum_tensor("ST", [8, 1024], F32))
    PA = ctxP4.enter_context(nc.psum_tensor("PA", [128, 1024], F32))
    PB = ctxP4.enter_context(nc.psum_tensor("PB", [128, 1024], F32))
    YY = ctxP4.enter_context(nc.psum_tensor("YY", [128, 1024], F32))

    wait("sync", "gp", tok_w1_cv)
    wait("sync", "gp", tok_w2_cv)

    preps = {}
    prep_pending = [None]

    def make_prep(sl):
        par = sl % 2
        xs_img = x1s[par][:, :].rearrange("p (c t) -> p c t", c=NCH)
        st = {}

        def stage_load():
            bwait("sync", f"x1s{par}")
            st["t_x1"] = dma("sync", f"dx1_{par}", x1s[par][:, :].rearrange("p (c t) -> p c t", c=NCH),
                             sc_x1[:, :, sl * 512:(sl + 1) * 512].rearrange("c p t -> p c t"))

        xbw = xn2s[par][:, :]
        sqw = cwb[:, :]

        def stage_xb():
            wait("vector", f"dx1_{par}", st["t_x1"])
            bwait("vector", f"xn2s{par}")
            bwait("vector", "cwbK")
            t = None
            for c in range(NCH):
                t = emit("vector", lambda e, dst=xbw[:, c * 512:(c + 1) * 512], src=xs_img[:, c, :]:
                         e.tensor_copy(dst, src), inc="dve")
                wait("vector", "dve", t)
                t = emit("vector", lambda e, dst=sqw[:, c * 512:(c + 1) * 512],
                         src=xbw[:, c * 512:(c + 1) * 512]:
                         e.tensor_mul(dst, src, src), inc="dve")
                wait("vector", "dve", t)
            st["t_xb"] = t

        def stage_stats():
            wait("tensor", "dve", tok_ones)
            wait("tensor", "dve", st["t_xb"])
            t_mm = None
            bwait("tensor", "ST")
            for c in range(NCH):
                emit("tensor", lambda e, dst=ST[0:1, 0:512], src=xbw[:, c * 512:(c + 1) * 512], c=c:
                     e.matmul(dst, ones_b[:, :], src, start=(c == 0), stop=(c == NCH - 1)))
                t_mm = emit("tensor", lambda e, dst=ST[0:1, 512:1024], src=sqw[:, c * 512:(c + 1) * 512], c=c:
                            e.matmul(dst, ones_b[:, :], src, start=(c == 0), stop=(c == NCH - 1)),
                            inc="pe")
            bfree("cwbK2", "pe", t_mm)
            st["t_mm"] = t_mm
            wait("vector", "pe", t_mm)
            bwait("vector", "lnrows")
            t = emit("vector", lambda e: e.tensor_scalar_mul(m_row[:, :], ST[0:1, 0:512], 1.0 / D), inc="dve")
            wait("vector", "dve", t)
            t = emit("vector", lambda e: e.tensor_scalar_mul(v_row[:, :], ST[0:1, 512:1024], 1.0 / D), inc="dve")
            wait("vector", "dve", t)
            t = emit("vector", lambda e: e.tensor_mul(t_row[:, :], m_row[:, :], m_row[:, :]), inc="dve")
            wait("vector", "dve", t)
            t_var = emit("vector", lambda e: e.tensor_sub(v_row[:, :], v_row[:, :], t_row[:, :]), inc="dve")
            bfree("ST", "dve", t_var)
            wait("scalar", "dve", t_var)
            wait("scalar", "dve", tok_eps)
            t = emit("scalar", lambda e: e.activation(t_row[:, :], v_row[:, :], AF.Ln, bias=EPS), inc="act")
            wait("scalar", "act", t)
            st["t_r"] = emit("scalar", lambda e: e.activation(r_row[:, :], t_row[:, :], AF.Exp, scale=-0.5), inc="act")

        def stage_bcast():
            wait("tensor", "act", st["t_r"])
            bwait("tensor", "PBr")
            emit("tensor", lambda e: e.matmul(PB[:, 0:512], orow_f[:, :], m_row[:, :], start=True, stop=True))
            t_bc = emit("tensor", lambda e: e.matmul(PB[:, 512:1024], orow_f[:, :], r_row[:, :], start=True, stop=True), inc="pe")
            bfree("lnrows", "pe", t_bc)
            wait("vector", "pe", t_bc)
            bwait("vector", f"xn2s{par}")
            wait("vector", "pe", st["t_mm"])
            t_ap = None
            for c in range(NCH):
                t = emit("vector", lambda e, dst=t1sC[:, :], src=xs_img[:, c, :]:
                         e.tensor_sub(dst, src, PB[:, 0:512]), inc="dve")
                wait("vector", "dve", t)
                t_ap = emit("vector", lambda e, dst=xn2s[par][:, c * 512:(c + 1) * 512], src=t1sC[:, :]:
                            e.tensor_mul(dst, src, PB[:, 512:1024]), inc="dve")
                wait("vector", "dve", t_ap)
            st["t_ap"] = t_ap

        def stage_raw():
            wait("tensor", "dve", st["t_ap"])
            wait("tensor", "dve", tok_rwb)
            bwait("tensor", "ST")
            t_raw = None
            for c in range(NCH):
                t_raw = emit("tensor", lambda e, dst=ST[0:8, 0:512],
                             l=rwTb[:, c * 8:(c + 1) * 8],
                             r=xn2s[par][:, c * 512:(c + 1) * 512], c=c:
                             e.matmul(dst, l, r, start=(c == 0), stop=(c == NCH - 1)),
                             inc=("pe" if c == NCH - 1 else None))
            wait("scalar", "pe", t_raw)
            t_eN = emit("scalar", lambda e: e.activation(prb[:, :], ST[0:8, 0:512], AF.Exp, scale=-1.0), inc="act")
            sch.free_tok["ST"] = ("act", t_eN)
            st["t_probs"] = ("act", t_eN)

        def stage_ptp():
            wait("tensor", st["t_probs"][0], st["t_probs"][1])
            wait("tensor", "dc", tok_dc_all)
            t_tp = None
            for tt in range(4):
                t_tp = emit("tensor", lambda e, dst=PB[:, 512 + tt * 8: 512 + tt * 8 + 8],
                            src=prb[:, tt * 128:(tt + 1) * 128]:
                            e.transpose(dst, src, idf[0:8, 0:8]),
                            inc=("pe" if tt == 3 else None))
            wait("vector", "pe", t_tp)
            t = emit("vector", lambda e: e.tensor_copy(pn[:, :], PB[:, 512:544]), inc="dve")
            wait("vector", "dve", t)
            t = emit("vector", lambda e: e.tensor_scalar_add(pn[:, :], pn[:, :], 1.0), inc="dve")
            wait("vector", "dve", t)
            t = emit("vector", lambda e: e.reciprocal(pn[:, :], pn[:, :]), inc="dve")
            wait("vector", "dve", t)
            t_cw = None
            for tt in range(4):
                sl8 = slice(tt * 8, tt * 8 + 8)
                def ch(fn):
                    tt_ = emit("vector", fn, inc="dve")
                    wait("vector", "dve", tt_)
                    return tt_
                ch(lambda e, sl8=sl8: e.reduce_max(m1c[:, :], pn[:, sl8], axis=AX.X))
                ch(lambda e, sl8=sl8: e.tensor_scalar(tmp8[:, sl8], pn[:, sl8], m1c[:, :], None, ALU.is_ge))
                ch(lambda e, sl8=sl8: e.tensor_scalar_mul(tmp8[:, sl8], tmp8[:, sl8], -1e9))
                ch(lambda e, sl8=sl8: e.tensor_add(tmp8[:, sl8], tmp8[:, sl8], pn[:, sl8]))
                ch(lambda e, sl8=sl8: e.reduce_max(m2c[:, :], tmp8[:, sl8], axis=AX.X))
                ch(lambda e: e.tensor_add(zc[:, :], m1c[:, :], m2c[:, :]))
                ch(lambda e: e.tensor_scalar_add(zc[:, :], zc[:, :], 1e-20))
                ch(lambda e: e.reciprocal(zc[:, :], zc[:, :]))
                ch(lambda e, sl8=sl8: e.tensor_scalar(tmp8[:, sl8], pn[:, sl8], m2c[:, :], None, ALU.is_ge))
                ch(lambda e, sl8=sl8: e.tensor_mul(tmp8[:, sl8], tmp8[:, sl8], pn[:, sl8]))
                t_cw = ch(lambda e, sl8=sl8: e.tensor_scalar(cwf[:, sl8], tmp8[:, sl8], zc[:, :], None, ALU.mult))
            st["t_cw"] = t_cw

        def stage_cwb():
            wait("tensor", "dve", st["t_cw"])
            t_tb2 = None
            for tt in range(4):
                t_tb2 = emit("tensor", lambda e, dst=PB[0:8, 512 + tt * 128: 512 + (tt + 1) * 128],
                             src=cwf[:, tt * 8:(tt + 1) * 8]:
                             e.transpose(dst, src, idf[:, :]),
                             inc=("pe" if tt == 3 else None))
            wait("vector", "pe", t_tb2)
            t_cwT = emit("vector", lambda e: e.tensor_copy(cwT[:, :], PB[0:8, 512:1024]), inc="dve")
            bfree("PBr", "dve", t_cwT)
            wait("scalar", "dve", t_cwT)
            bwait("scalar", "dsccw")
            t_s1 = dma("scalar", "dcw", sc_cw[:, :], cwT[:, :])
            wait("scalar", "dcw", t_s1)
            bwait("scalar", "cwbK")
            bwait("scalar", "cwbK2")
            t_s2 = dma("scalar", "dcw",
                       cwb[:, :].rearrange("p (e t) -> p e t", e=E),
                       bass.AP(sc_cw.tensor, 0, [[0, 128], [512, E], [1, 512]]))
            sch.free_tok["dsccw"] = ("dcw", t_s2)
            preps[sl] = (par, t_s2)

        return [stage_load, stage_xb, stage_stats, stage_bcast, stage_raw, stage_ptp, stage_cwb]

    for f in make_prep(0):
        f()
    prep_pending[0] = None
    prev_r_last = None
    prev_a_last = None
    for sl in range(NSLAB):
        par, t_ready = preps[sl]

        r_tok = {-2: prev_r_last, -1: prev_r_last}
        a_tok = {-2: prev_a_last, -1: prev_a_last}
        h_last = None
        for J in range(NCH):
            parw = J % 2
            bwait("gpsimd", f"w1buf{parw}")
            t_w1 = dma("gpsimd", f"dw1_{parw}", w1buf[parw][:, :], sc_w1b[J, :, :])
            wait("tensor", f"dw1_{parw}", t_w1)
            if J == 0:
                wait("tensor", "dcw", t_ready)
            for u in range(NCH):
                j = 8 * J + u
                parh = j % 2
                ee = j // 8
                wait("tensor", "act", r_tok.get(j - 2))
                t_h = None
                for c in range(NCH):
                    t_h = emit("tensor", lambda e, dst=PA[:, parh * 512:(parh + 1) * 512],
                               l=w1buf[parw][:, u * 1024 + c * 128: u * 1024 + c * 128 + 128],
                               r=xn2s[par][:, c * 512:(c + 1) * 512], c=c:
                               e.matmul(dst, l, r, start=(c == 0), stop=(c == NCH - 1)),
                               inc=("pe" if c == NCH - 1 else None))
                h_last = t_h
                wait("scalar", "pe", t_h)
                wait("scalar", "dve", a_tok.get(j - 2))
                wait("scalar", "act", r_tok.get(j - 2))
                t_r = emit("scalar", lambda e, dst=rbuf[parh][:, :],
                           src=PA[:, parh * 512:(parh + 1) * 512]:
                           e.activation(dst, src, AF.Relu), inc="act")
                r_tok[j] = t_r
                wait("vector", "act", t_r)
                bwait("vector", "aT")
                t = emit("vector", lambda e, d=tbuf[parh][:, :], s=rbuf[parh][:, :]:
                         e.tensor_mul(d, s, s), inc="dve")
                wait("vector", "dve", t)
                t_a = emit("vector", lambda e, d=aT[:, j * 512:(j + 1) * 512],
                           s=tbuf[parh][:, :], w=cwb[:, ee * 512:(ee + 1) * 512]:
                           e.tensor_mul(d, s, w), inc="dve")
                a_tok[j] = t_a
            bfree(f"w1buf{parw}", "pe", h_last)
        bfree("PAbc", "act", r_tok[NJ - 1])
        bfree("cwbK", "dve", a_tok[NJ - 1])
        sch.free_tok["PBr2"] = ("dve", a_tok[NJ - 1])
        bfree(f"xn2s{par}", "pe", h_last)
        prev_r_last = r_tok[NJ - 1]
        prev_a_last = a_tok[NJ - 1]

        prep_stages = make_prep(sl + 1) if sl + 1 < NSLAB else []
        slot_map = {0: 0, 1: 1, 2: 2, 3: 3, 4: 4, 5: 5, 6: 6}
        y_last = None
        for mt in range(8):
            if mt in slot_map and slot_map[mt] < len(prep_stages):
                prep_stages[slot_map[mt]]()
            for hf in range(2):
                parw = (2 * mt + hf) % 3
                bwait("sync", f"w2buf{parw}")
                t_w2 = dma("sync", f"dw2_{parw}", w2buf[parw][:, :],
                           sc_w2b[mt, :, hf * 4096:(hf + 1) * 4096])
                wait("tensor", f"dw2_{parw}", t_w2)
                if hf == 0:
                    wait("tensor", "dve", a_tok[NJ // 2 - 1])
                    bwait("tensor", f"yy{mt % 2}")
                else:
                    wait("tensor", "dve", a_tok[NJ - 1])
                t_y = None
                for jj in range(NJ // 2):
                    jt = hf * 32 + jj
                    t_y = emit("tensor", lambda e, dst=YY[:, (mt % 2) * 512:(mt % 2 + 1) * 512],
                               l=w2buf[parw][:, jj * 128:(jj + 1) * 128],
                               r=aT[:, jt * 512:(jt + 1) * 512], jt=jt:
                               e.matmul(dst, l, r, start=(jt == 0), stop=(jt == NJ - 1)),
                               inc=("pe" if jj == NJ // 2 - 1 else None))
                bfree(f"w2buf{parw}", "pe", t_y)
            wait("vector", "pe", t_y)
            bwait("vector", f"ostg{mt % 2}")
            t_o = emit("vector", lambda e, d=ostg[mt % 2][:, :],
                       a=YY[:, (mt % 2) * 512:(mt % 2 + 1) * 512],
                       b=x1s[par][:, mt * 512:(mt + 1) * 512]:
                       e.tensor_add(d, a, b), inc="dve")
            bfree(f"yy{mt % 2}", "dve", t_o)
            y_last = t_o
            wait("scalar", "dve", t_o)
            t_out = dma("scalar", f"dos{mt % 2}", outT[mt * 128:(mt + 1) * 128, sl * 512:(sl + 1) * 512],
                        ostg[mt % 2][:, :])
            bfree(f"ostg{mt % 2}", f"dos{mt % 2}", t_out)
        bfree(f"x1s{par}", "dve", y_last)
        bfree("aT", "pe", sch.cnt["pe"])

    sch.barrier()

    for name in sorted(sch.sems):
        semh[name] = ctxO.enter_context(nc.semaphore(name))
    sch.play(nc, semh)
    ctxP4.close()
    ctxC.close()
    ctxO.close()
    return nc


_MASKTRI = np.triu(np.ones((128, 128), np.float32)).astype(bfloat16)
_IDENT = np.eye(128, dtype=np.float32)


def kernel(L0_attn_x, L0_attn_Wq, L0_attn_Wk, L0_attn_Wv, L0_attn_Wo, L0_attn_bo,
           L0_ffn_router_w, L0_ffn_w1, L0_ffn_w2):
    x = np.asarray(L0_attn_x, dtype=np.float32)
    B, S, _ = x.shape
    nc = build(S)
    shared = {
        "wq": np.ascontiguousarray(np.asarray(L0_attn_Wq, np.float32)),
        "wk": np.ascontiguousarray(np.asarray(L0_attn_Wk, np.float32)),
        "wv": np.ascontiguousarray(np.asarray(L0_attn_Wv, np.float32)),
        "wo": np.ascontiguousarray(np.asarray(L0_attn_Wo, np.float32)),
        "bo": np.ascontiguousarray(np.asarray(L0_attn_bo, np.float32).reshape(NCH, 128).T),
        "rwT": np.ascontiguousarray(np.asarray(L0_ffn_router_w, np.float32).T),
        "w1": np.ascontiguousarray(np.asarray(L0_ffn_w1, np.float32)),
        "w2": np.ascontiguousarray(np.asarray(L0_ffn_w2, np.float32)),
        "identf": _IDENT,
        "masktri": _MASKTRI,
    }
    in_maps = []
    for c in range(B):
        m = dict(shared)
        m["xT"] = np.ascontiguousarray(x[c].T)
        in_maps.append(m)
    res = run_bass_kernel_spmd(nc, in_maps, core_ids=list(range(B)))
    out = np.stack([np.asarray(res.results[c]["outT"]).T for c in range(B)], axis=0)
    return out.astype(np.float32)


if __name__ == "__main__":
    nc = build(int(sys.argv[1]) if len(sys.argv) > 1 else 512)
    print("build ok")


# version 27 (final)
# speedup vs baseline: 1.0278x; 1.0278x over previous
import sys
sys.path.insert(0, "/opt/trn_rl_repo")
import numpy as np
import ml_dtypes
from contextlib import ExitStack
import concourse.bass as bass
import concourse.mybir as mybir
from concourse.bass_utils import run_bass_kernel_spmd

F32 = mybir.dt.float32
BF16 = mybir.dt.bfloat16
AF = mybir.ActivationFunctionType
ALU = mybir.AluOpType
AX = mybir.AxisListType
bfloat16 = ml_dtypes.bfloat16

D = 1024
NCH = 8
NH = 16
NPAIR = 8
E = 8
W_FF = 8192
NJ = 64
EPS = 1e-5

ENGINES = ("sync", "tensor", "vector", "scalar", "gpsimd")


class Sched:
    def __init__(self):
        from collections import defaultdict
        self.ops = {e: [] for e in ENGINES}
        self.cnt = defaultdict(int)
        self.waited = {e: defaultdict(int) for e in ENGINES}
        self.sems = set(("pe", "act", "dve", "gp"))
        self.free_tok = {}

    def emit(self, eng, fn, inc=None, amt=1):
        tok = None
        if inc is not None:
            self.sems.add(inc)
            self.cnt[inc] += amt
            tok = self.cnt[inc]
        self.ops[eng].append((fn, inc, amt))
        return tok

    def wait(self, eng, sem, tok):
        if tok is None or tok <= 0:
            return
        if self.waited[eng][sem] >= tok:
            return
        self.waited[eng][sem] = tok
        self.ops[eng].append((None, (sem, tok), 0))

    def bwait(self, eng, key):
        if key in self.free_tok:
            s, t = self.free_tok[key]
            self.wait(eng, s, t)

    def bfree(self, key, sem, tok):
        self.free_tok[key] = (sem, tok)

    def barrier(self, sems=None):
        for eng in ENGINES:
            for s in sorted(self.sems if sems is None else sems):
                self.wait(eng, s, self.cnt[s])

    def play(self, nc, semh):
        assert set(semh) >= self.sems, self.sems - set(semh)
        with nc.Block() as block:
            def body(eng_name):
                def run(engobj):
                    for fn, inc, amt in self.ops[eng_name]:
                        if fn is None:
                            s, tok = inc
                            engobj.wait_ge(semh[s], tok)
                        else:
                            instr = fn(engobj)
                            if inc is not None:
                                instr.then_inc(semh[inc], amt)
                return run
            block.sync(body("sync"))
            block.tensor(body("tensor"))
            block.vector(body("vector"))
            block.scalar(body("scalar"))
            block.gpsimd(body("gpsimd"))


def build(S=4096):
    NTB = S // 512
    NKT = S // 128
    NQB = S // 512
    NSLAB = S // 512

    nc = bass.Bass("TRN2", target_bir_lowering=False)
    sch = Sched()
    emit, wait, bwait, bfree = sch.emit, sch.wait, sch.bwait, sch.bfree

    xT = nc.dram_tensor("xT", [D, S], F32, kind="ExternalInput").ap()
    wq = nc.dram_tensor("wq", [D, D], F32, kind="ExternalInput").ap()
    wk = nc.dram_tensor("wk", [D, D], F32, kind="ExternalInput").ap()
    wv = nc.dram_tensor("wv", [D, D], F32, kind="ExternalInput").ap()
    wo = nc.dram_tensor("wo", [D, D], F32, kind="ExternalInput").ap()
    bo = nc.dram_tensor("bo", [128, NCH], F32, kind="ExternalInput").ap()
    rwT = nc.dram_tensor("rwT", [D, E], F32, kind="ExternalInput").ap()
    w1 = nc.dram_tensor("w1", [D, W_FF], F32, kind="ExternalInput").ap()
    w2 = nc.dram_tensor("w2", [W_FF, D], F32, kind="ExternalInput").ap()
    identf = nc.dram_tensor("identf", [128, 128], F32, kind="ExternalInput").ap()
    masktri = nc.dram_tensor("masktri", [128, 128], BF16, kind="ExternalInput").ap()
    outT = nc.dram_tensor("outT", [D, S], F32, kind="ExternalOutput").ap()

    sc_wqb = nc.dram_tensor("sc_wqb", [NCH, 128, D], BF16, kind="Internal").ap()
    sc_wkb = nc.dram_tensor("sc_wkb", [NCH, 128, D], BF16, kind="Internal").ap()
    sc_wvb = nc.dram_tensor("sc_wvb", [NCH, 128, D], BF16, kind="Internal").ap()
    sc_wob = nc.dram_tensor("sc_wob", [NCH, 128, D], BF16, kind="Internal").ap()
    sc_w1b = nc.dram_tensor("sc_w1b", [NCH, 128, W_FF], BF16, kind="Internal").ap()
    sc_w2b = nc.dram_tensor("sc_w2b", [NCH, 128, W_FF], BF16, kind="Internal").ap()
    sc_attn = nc.dram_tensor("sc_attn", [NCH, 128, S], BF16, kind="Internal").ap()
    sc_z = nc.dram_tensor("sc_z", [NPAIR, 2, S], F32, kind="Internal").ap()
    sc_z2 = nc.dram_tensor("sc_z2", [NPAIR, 2, S], BF16, kind="Internal").ap()
    sc_cw = nc.dram_tensor("sc_cw", [8, 512], BF16, kind="Internal").ap()
    sc_x1 = nc.dram_tensor("sc_x1", [NCH, 128, S], F32, kind="Internal").ap()

    ctxO = ExitStack()
    semh = {}

    def gate(eng, qsems, note=""):
        for q in qsems:
            wait(eng, q, sch.cnt[q])
        return emit(eng, lambda e: e.nop(), inc="gp")

    def sbO(name, shape, dt):
        return ctxO.enter_context(nc.sbuf_tensor(name, shape, dt))

    idf = sbO("idf", [128, 128], F32)
    mtri = sbO("mtri", [128, 128], BF16)
    ones_b = sbO("ones_b", [128, 1], BF16)
    ones_c = sbO("ones_c", [128, 1], F32)
    orow_f = sbO("orow_f", [1, 128], F32)
    orow_b = sbO("orow_b", [1, 128], BF16)
    orow_b64 = sbO("orow_b64", [65, 128], BF16)
    bo_sb = sbO("bo_sb", [128, NCH], F32)
    rwTb = sbO("rwTb", [128, E * NCH], BF16)
    m_row = sbO("m_row", [1, 512], F32)
    v_row = sbO("v_row", [1, 512], F32)
    r_row = sbO("r_row", [1, 512], F32)
    t_row = sbO("t_row", [1, 512], F32)

    def dma(eng, q, dst, src):
        return emit(eng, lambda e, d=dst, s=src: e.dma_start(d, s), inc=q, amt=16)

    dma("sync", "dc", idf[:, :], identf[:, :])
    dma("sync", "dc", mtri[:, :], masktri[:, :])
    dma("sync", "dc", bo_sb[:, :], bo[:, :])
    emit("vector", lambda e: e.memset(ones_b[:, :], 1.0), inc="dve")
    emit("vector", lambda e: e.memset(ones_c[:, :], 1.0), inc="dve")
    emit("vector", lambda e: e.memset(orow_f[:, :], 1.0), inc="dve")
    emit("vector", lambda e: e.memset(orow_b[:, :], 1.0), inc="dve")
    tok_ones = emit("vector", lambda e: e.memset(orow_b64[64:65, :], 1.0), inc="dve")

    eps_sb = sbO("eps_sb", [1, 1], F32)
    tok_eps = emit("vector", lambda e: e.memset(eps_sb[:, :], EPS), inc="dve")
    nc.const_aps.register(F32, EPS, eps_sb[0:1, 0:1])
    rwf = sbO("rwf", [128, E * NCH], F32)
    tok_dc = dma("sync", "dc", rwf[:, :].rearrange("p (c e) -> p c e", c=NCH),
                 rwT.rearrange("(c p) e -> p c e", p=128))
    wait("vector", "dc", tok_dc)
    tok_rwb = emit("vector", lambda e: e.tensor_copy(rwTb[:, :], rwf[:, :]), inc="dve")
    tok_dc_all = sch.cnt["dc"]

    def ln_block(xs_img, out_img_fn, st_ps, st_key, bc_ps, bc_key, xb2, sq2, t1b):
        wait("tensor", "dve", tok_ones)
        t_mm = None
        for c in range(NCH):
            par = c % 2
            xbb, sqb = xb2[par], sq2[par]
            bwait("vector", f"xb:{xbb.name}")
            t_xb = emit("vector", lambda e, dst=xbb[:, :], src=xs_img[:, c, :]:
                        e.tensor_copy(dst, src), inc="dve")
            wait("vector", "dve", t_xb)
            bwait("vector", f"sq:{sqb.name}")
            t_sq = emit("vector", lambda e, dst=sqb[:, :], s=xbb[:, :]:
                        e.tensor_mul(dst, s, s), inc="dve")
            wait("tensor", "dve", t_sq)
            if c == 0:
                bwait("tensor", st_key)
            emit("tensor", lambda e, dst=st_ps[0:1, 0:512], s=xbb[:, :], c=c:
                 e.matmul(dst, ones_b[:, :], s, start=(c == 0), stop=(c == NCH - 1)))
            t_mm = emit("tensor", lambda e, dst=st_ps[0:1, 512:1024], s=sqb[:, :], c=c:
                        e.matmul(dst, ones_b[:, :], s, start=(c == 0), stop=(c == NCH - 1)),
                        inc="pe")
            bfree(f"xb:{xbb.name}", "pe", t_mm)
            bfree(f"sq:{sqb.name}", "pe", t_mm)
        wait("vector", "pe", t_mm)
        bwait("vector", "lnrows")
        t = emit("vector", lambda e, s=st_ps[0:1, 0:512]: e.tensor_scalar_mul(m_row[:, :], s, 1.0 / D), inc="dve")
        wait("vector", "dve", t)
        t = emit("vector", lambda e, s=st_ps[0:1, 512:1024]: e.tensor_scalar_mul(v_row[:, :], s, 1.0 / D), inc="dve")
        wait("vector", "dve", t)
        t = emit("vector", lambda e: e.tensor_mul(t_row[:, :], m_row[:, :], m_row[:, :]), inc="dve")
        wait("vector", "dve", t)
        t_var = emit("vector", lambda e: e.tensor_sub(v_row[:, :], v_row[:, :], t_row[:, :]), inc="dve")
        bfree(st_key, "dve", t_var)
        wait("scalar", "dve", t_var)
        wait("scalar", "dve", tok_eps)
        t = emit("scalar", lambda e: e.activation(t_row[:, :], v_row[:, :], AF.Ln, bias=EPS), inc="act")
        wait("scalar", "act", t)
        t_r = emit("scalar", lambda e: e.activation(r_row[:, :], t_row[:, :], AF.Exp, scale=-0.5), inc="act")
        wait("tensor", "act", t_r)
        bwait("tensor", bc_key)
        emit("tensor", lambda e, dst=bc_ps[:, 0:512]: e.matmul(dst, orow_f[:, :], m_row[:, :], start=True, stop=True))
        t_bc = emit("tensor", lambda e, dst=bc_ps[:, 512:1024]: e.matmul(dst, orow_f[:, :], r_row[:, :], start=True, stop=True), inc="pe")
        bfree("lnrows", "pe", t_bc)
        wait("vector", "pe", t_bc)
        t_ap = None
        for c in range(NCH):
            t = emit("vector", lambda e, dst=t1b[:, :], src=xs_img[:, c, :], mb=bc_ps[:, 0:512]:
                     e.tensor_sub(dst, src, mb), inc="dve")
            wait("vector", "dve", t)
            t_ap = emit("vector", lambda e, dst=out_img_fn(c), src=t1b[:, :], rb=bc_ps[:, 512:1024]:
                        e.tensor_mul(dst, src, rb), inc="dve")
            wait("vector", "dve", t_ap)
        bfree(bc_key, "dve", t_ap)
        return t_ap

    ctxA = ExitStack()

    def sbA(name, shape, dt):
        return ctxA.enter_context(nc.sbuf_tensor(name, shape, dt))

    cvf = [sbA(f"cvf{i}", [128, NCH * 128], F32) for i in range(2)]
    cvb = [sbA("cvb0", [128, NCH * 128], BF16)] * 2
    xnT = sbA("xnT", [128, NCH * S], BF16)
    Vst = sbA("Vst", [128, NKT * NH * 65], BF16)


    cv_par = [0]
    cv_eng = ["gpsimd"]

    def conv_chunk(srcap, dstap):
        ce = cv_eng[0]
        csem = "act" if ce == "scalar" else "gp"
        par = cv_par[0] % 2
        cv_par[0] += 1
        fb = cvf[par][:, :].rearrange("p (c j) -> p c j", c=NCH)
        bb = cvb[par][:, :].rearrange("p (c j) -> p c j", c=NCH)
        bwait("gpsimd", f"cvf{par}")
        t_ld = dma("gpsimd", f"dcl{par}", fb, srcap)
        wait(ce, f"dcl{par}", t_ld)
        bwait(ce, "cvb")
        t_cv = emit(ce, lambda e, bb=bb, fb=fb:
                    (e.copy(bb, fb) if ce == "scalar" else e.tensor_copy(bb, fb)), inc=csem)
        bfree(f"cvf{par}", csem, t_cv)
        wait("gpsimd", csem, t_cv)
        t_st = dma("gpsimd", f"dcs{par}", dstap, bb)
        bfree("cvb", f"dcs{par}", t_st)
        return t_st

    def conv_square(w_ap, sc_ap):
        for i in range(8):
            a = i * 128
            conv_chunk(w_ap[:, a:a + 128].rearrange("(c p) j -> p c j", p=128),
                       sc_ap[:, :, a:a + 128].rearrange("c p j -> p c j"))
        return gate("gpsimd", ["dcs0", "dcs1"])

    def conv_w1_chunk(i):
        a = i * 128
        J, u = i // 8, i % 8
        dst = sc_w1b[J, :, u * 1024:(u + 1) * 1024].rearrange("p (c j) -> p c j", c=NCH)
        return conv_chunk(w1[:, a:a + 128].rearrange("(c p) j -> p c j", p=128), dst)

    def conv_w2_chunk(i):
        ce = cv_eng[0]
        csem = "act" if ce == "scalar" else "gp"
        C, mi = i // 8, i % 8
        m0 = mi * 128
        par = cv_par[0] % 2
        cv_par[0] += 1
        fb = cvf[par][:, :].rearrange("p (js m) -> p js m", js=8)
        bb = cvb[par][:, :].rearrange("p (js m) -> p js m", js=8)
        bwait("gpsimd", f"cvf{par}")
        t_ld = dma("gpsimd", f"dcl{par}", fb,
                   w2[C * 1024:(C + 1) * 1024, m0:m0 + 128].rearrange("(js p) m -> p js m", p=128))
        wait(ce, f"dcl{par}", t_ld)
        bwait(ce, "cvb")
        t_cv = emit(ce, lambda e, bb=bb, fb=fb:
                    (e.copy(bb, fb) if ce == "scalar" else e.tensor_copy(bb, fb)), inc=csem)
        bfree(f"cvf{par}", csem, t_cv)
        wait("gpsimd", csem, t_cv)
        MT = mi
        dst = sc_w2b[MT, :, :].rearrange("p (jt m) -> p jt m", jt=NJ)[:, C * 8:(C + 1) * 8, 0:128]
        t_st = dma("gpsimd", f"dcs{par}", dst, bb[:, :, 0:128])
        bfree("cvb", f"dcs{par}", t_st)
        return t_st

    tok_wv_cv = conv_square(wv, sc_wvb)
    tok_wq_cv = conv_square(wq, sc_wqb)
    tok_wk_cv = conv_square(wk, sc_wkb)

    ctxP01 = ExitStack()
    stat_ps = ctxP01.enter_context(nc.psum_tensor("stat_ps", [1, 1024], F32))
    bc_ps = ctxP01.enter_context(nc.psum_tensor("bc_ps", [128, 1024], F32))
    vp_ps = [ctxP01.enter_context(nc.psum_tensor(f"vp{i}", [128, 1024], F32)) for i in range(2)]

    ctxA0 = ExitStack()
    def sbA0(name, shape, dt):
        return ctxA0.enter_context(nc.sbuf_tensor(name, shape, dt))
    x0s = [sbA0(f"x0s{i}", [128, NCH * 512], F32) for i in range(2)]
    sqtA = [sbA0(f"sqtA{i}", [128, 512], F32) for i in range(2)]
    t1sA = sbA0("t1sA", [128, 512], F32)
    wvb = sbA0("wvb", [128, NCH * D], BF16)

    bg_conv = ([("wo", i) for i in range(8)] + [("w1", i) for i in range(64)]
               + [("w2", i) for i in range(64)])
    bg_done = {}

    def run_bg_conv(n):
        for _ in range(n):
            if not bg_conv:
                return
            kind, i = bg_conv.pop(0)
            if kind == "wo":
                a = i * 128
                conv_chunk(wo[:, a:a + 128].rearrange("(c p) j -> p c j", p=128),
                           sc_wob[:, :, a:a + 128].rearrange("c p j -> p c j"))
            elif kind == "w1":
                conv_w1_chunk(i)
            else:
                conv_w2_chunk(i)
            if not bg_conv or bg_conv[0][0] != kind:
                bg_done[kind] = gate("gpsimd", ["dcs0", "dcs1"])

    t_vones = emit("vector", lambda e:
                   e.memset(Vst[:, :].rearrange("p (k h w) -> p k h w", k=NKT, h=NH)[:, :, :, 64:65], 1.0),
                   inc="dve")
    wait("sync", "gp", tok_wv_cv)
    t_wvb = dma("sync", "dwv", wvb[:, :].rearrange("p (c i) -> p c i", c=NCH),
                sc_wvb.rearrange("c p i -> p c i"))
    wait("tensor", "dwv", t_wvb)
    wait("tensor", "dve", t_vones)

    v_done = None
    lnst = {}

    def p0_dma(t):
        par = t % 2
        bwait("sync", f"x0s{par}")
        lnst[("x", t)] = dma("sync", f"dx0_{par}",
                             x0s[par][:, :].rearrange("p (c t) -> p c t", c=NCH),
                             xT[:, t * 512:(t + 1) * 512].rearrange("(c p) t -> p c t", p=128))

    def p0_stats(t):
        par = t % 2
        xs = x0s[par][:, :].rearrange("p (c t) -> p c t", c=NCH)
        wait("vector", f"dx0_{par}", lnst[("x", t)])
        wait("tensor", "dve", tok_ones)
        t_mm = None
        bwait("tensor", "stat_ps")
        for c in range(NCH):
            pp = c % 2
            bwait("vector", f"sq:{sqtA[pp].name}")
            t_sq = emit("vector", lambda e, dst=sqtA[pp][:, :], src=xs[:, c, :]:
                        e.tensor_mul(dst, src, src), inc="dve")
            wait("vector", "dve", t_sq)
            wait("tensor", f"dx0_{par}", lnst[("x", t)])
            emit("tensor", lambda e, src=xs[:, c, :], c=c:
                 e.matmul(stat_ps[0:1, 0:512], ones_c[:, :], src, start=(c == 0), stop=(c == NCH - 1)))
            wait("tensor", "dve", t_sq)
            t_mm = emit("tensor", lambda e, src=sqtA[pp][:, :], c=c:
                        e.matmul(stat_ps[0:1, 512:1024], ones_c[:, :], src, start=(c == 0), stop=(c == NCH - 1)),
                        inc="pe")
            bfree(f"sq:{sqtA[pp].name}", "pe", t_mm)
        lnst[("mm", t)] = t_mm

    def p0_fin(t):
        wait("vector", "pe", lnst[("mm", t)])
        bwait("vector", "lnrows")
        tt = emit("vector", lambda e: e.tensor_scalar_mul(m_row[:, :], stat_ps[0:1, 0:512], 1.0 / D), inc="dve")
        wait("vector", "dve", tt)
        tt = emit("vector", lambda e: e.tensor_scalar_mul(v_row[:, :], stat_ps[0:1, 512:1024], 1.0 / D), inc="dve")
        wait("vector", "dve", tt)
        tt = emit("vector", lambda e: e.tensor_mul(t_row[:, :], m_row[:, :], m_row[:, :]), inc="dve")
        wait("vector", "dve", tt)
        t_var = emit("vector", lambda e: e.tensor_sub(v_row[:, :], v_row[:, :], t_row[:, :]), inc="dve")
        bfree("stat_ps", "dve", t_var)
        wait("scalar", "dve", t_var)
        wait("scalar", "dve", tok_eps)
        tt = emit("scalar", lambda e: e.activation(t_row[:, :], v_row[:, :], AF.Ln, bias=EPS), inc="act")
        wait("scalar", "act", tt)
        lnst[("r", t)] = emit("scalar", lambda e: e.activation(r_row[:, :], t_row[:, :], AF.Exp, scale=-0.5), inc="act")

    def p0_bcast(t):
        wait("tensor", "act", lnst[("r", t)])
        bwait("tensor", "bc_ps")
        emit("tensor", lambda e: e.matmul(bc_ps[:, 0:512], orow_f[:, :], m_row[:, :], start=True, stop=True))
        t_bc = emit("tensor", lambda e: e.matmul(bc_ps[:, 512:1024], orow_f[:, :], r_row[:, :], start=True, stop=True), inc="pe")
        bfree("lnrows", "pe", t_bc)
        lnst[("bc", t)] = t_bc

    def p0_apply(t):
        par = t % 2
        xs = x0s[par][:, :].rearrange("p (c t) -> p c t", c=NCH)
        wait("vector", "pe", lnst[("bc", t)])
        t_ap = None
        for c in range(NCH):
            tt = emit("vector", lambda e, dst=t1sA[:, :], src=xs[:, c, :]:
                      e.tensor_sub(dst, src, bc_ps[:, 0:512]), inc="dve")
            wait("vector", "dve", tt)
            t_ap = emit("vector", lambda e, dst=xnT[:, c * S + t * 512: c * S + (t + 1) * 512], src=t1sA[:, :]:
                        e.tensor_mul(dst, src, bc_ps[:, 512:1024]), inc="dve")
            wait("vector", "dve", t_ap)
        bfree("bc_ps", "dve", t_ap)
        bfree(f"x0s{par}", "dve", t_ap)
        lnst[("ap", t)] = t_ap

    def p0_v(t):
        nonlocal v_done
        wait("tensor", "dve", lnst[("ap", t)])
        for kt in range(4 * t, 4 * t + 4):
            vpar = kt % 2
            bwait("tensor", f"vp{vpar}")
            t_v = None
            for c in range(NCH):
                for h in range(2):
                    t_v = emit("tensor", lambda e, dst=vp_ps[vpar][:, h * 512:(h + 1) * 512],
                               l=xnT[:, c * S + kt * 128: c * S + kt * 128 + 128],
                               r=wvb[:, c * D + h * 512: c * D + (h + 1) * 512], c=c:
                               e.matmul(dst, l, r, start=(c == 0), stop=(c == NCH - 1)),
                               inc=("pe" if (c == NCH - 1 and h == 1) else None))
            wait("vector", "pe", t_v)
            dst = Vst[:, kt * 1040:(kt + 1) * 1040].rearrange("p (h w) -> p h w", h=NH)
            t_ev = emit("vector", lambda e, d=dst[:, :, 0:64],
                        s=vp_ps[vpar][:, :].rearrange("p (h w) -> p h w", h=NH, w=64):
                        e.tensor_copy(d, s), inc="dve")
            bfree(f"vp{vpar}", "dve", t_ev)
            v_done = t_ev

    p0_dma(0)
    p0_stats(0)
    p0_fin(0)
    for t in range(NTB):
        if t + 1 < NTB:
            p0_dma(t + 1)
        p0_bcast(t)
        if t + 1 < NTB:
            p0_stats(t + 1)
        p0_apply(t)
        if t + 1 < NTB:
            p0_fin(t + 1)
        p0_v(t)

    sch.barrier(sems=("pe", "act", "dve"))
    ctxP01.close()
    ctxA0.close()

    qkwq = [sbA(f"qkwq{i}", [128, NCH * 128], BF16) for i in range(2)]
    qkwk = [sbA(f"qkwk{i}", [128, NCH * 128], BF16) for i in range(2)]
    QT = sbA("QT", [128, S], BF16)
    KT = [sbA(f"KT{i}", [128, S], BF16) for i in range(2)]
    ptil = [sbA(f"ptil{i}", [128, 1024], BF16) for i in range(3)]
    SHALF = min(2048, S)
    QBH = SHALF // 512
    stgA = sbA("stgA", [64, SHALF], BF16)
    stgB = sbA("stgB", [64, SHALF], BF16)
    zraw = sbA("zraw", [65, 1024], F32)

    ctxP2 = ExitStack()
    s_ps = [ctxP2.enter_context(nc.psum_tensor(f"s_ps{i}", [128, 1024], F32)) for i in range(2)]
    pvA = ctxP2.enter_context(nc.psum_tensor("pvA", [128, 512], F32))
    pvB = ctxP2.enter_context(nc.psum_tensor("pvB", [128, 512], F32))
    qk_ps = [ctxP2.enter_context(nc.psum_tensor(f"qk_ps{i}", [128, 512], F32)) for i in range(2)]

    sexp_tok = {}
    exp_tok = {}
    pv_tok = {}
    gt = [0]
    ktq_tok = {}
    last_s_tok = {}

    def emit_qkw_load(p):
        pw = p % 2
        wait("scalar", "gp", tok_wq_cv)
        wait("scalar", "gp", tok_wk_cv)
        bwait("scalar", f"qkwq{pw}")
        dma("scalar", f"dqk{pw}", qkwq[pw][:, :].rearrange("p (c i) -> p c i", c=NCH),
            sc_wqb[:, :, p * 128:(p + 1) * 128].rearrange("c p i -> p c i"))
        bwait("scalar", f"qkwk{pw}")
        t_wk = dma("scalar", f"dqk{pw}", qkwk[pw][:, :].rearrange("p (c i) -> p c i", c=NCH),
                   sc_wkb[:, :, p * 128:(p + 1) * 128].rearrange("c p i -> p c i"))
        return t_wk

    qkw_tok = {}

    def emit_proj_group(p, tb, guard_pair):
        pw = p % 2
        wait("tensor", f"dqk{pw}", qkw_tok[p])
        t_evs = []
        for wbuf, bank, dstbuf, do_scale in ((qkwq[pw], 0, QT, True), (qkwk[pw], 1, KT[pw], False)):
            bwait("tensor", f"qk_ps{bank}")
            t_pr = None
            for c in range(NCH):
                t_pr = emit("tensor", lambda e, dst=qk_ps[bank][:, :],
                            l=wbuf[:, c * 128:(c + 1) * 128],
                            r=xnT[:, c * S + tb * 512: c * S + (tb + 1) * 512], c=c:
                            e.matmul(dst, l, r, start=(c == 0), stop=(c == NCH - 1)),
                            inc=("pe" if c == NCH - 1 else None))
            if do_scale and guard_pair is not None:
                wait("vector", "pe", last_s_tok[(guard_pair, tb)])
            if not do_scale:
                bwait("vector", f"KTbuf{pw}")
            wait("vector", "pe", t_pr)
            if do_scale:
                t_ev = emit("vector", lambda e, d=dstbuf[:, tb * 512:(tb + 1) * 512],
                            s=qk_ps[bank][:, :]: e.tensor_scalar_mul(d, s, 0.125), inc="dve")
            else:
                t_ev = emit("vector", lambda e, d=dstbuf[:, tb * 512:(tb + 1) * 512],
                            s=qk_ps[bank][:, :]: e.tensor_copy(d, s), inc="dve")
            bfree(f"qk_ps{bank}", "dve", t_ev)
            t_evs.append(t_ev)
            bfree("qkwq%d" % pw if do_scale else "qkwk%d" % pw, "pe", t_pr)
        ktq_tok[(p, tb)] = t_evs[-1]
        return t_evs

    pv_act_free = [None]

    wait("tensor", "dve", v_done)
    qkw_tok[0] = emit_qkw_load(0)
    for tb in range(NTB):
        emit_proj_group(0, tb, None)

    for p in range(NPAIR):
        pw = p % 2
        if p + 1 < NPAIR:
            qkw_tok[p + 1] = emit_qkw_load(p + 1)

        lazy_q = []
        t_ev = None

        def do_lazy_slot():
            if not lazy_q:
                return
            kind, arg = lazy_q.pop(0)
            emit_proj_group(p + 1, arg, p)

        for qb in range(NQB):
            nk = 4 * qb + 4
            tk0 = gt[0]
            wait("tensor", "dve", ktq_tok[(p, qb)])

            def emit_S(j):
                tj = tk0 + j
                par = tj % 2
                par3 = tj % 3
                r_ = j - 4 * qb
                off = 128 * r_ if r_ >= 0 else 0
                wait("tensor", "act", sexp_tok.get(tj - 2))
                emit("tensor", lambda e, dst=s_ps[par][:, off:512],
                     l=KT[pw][0:64, j * 128:(j + 1) * 128],
                     r=QT[0:64, qb * 512 + off:(qb + 1) * 512]:
                     e.matmul(dst, l, r, start=True, stop=True, tile_position=(0, 0)))
                t_s = emit("tensor", lambda e, dst=s_ps[par][:, 512 + off:1024],
                           l=KT[pw][64:128, j * 128:(j + 1) * 128],
                           r=QT[64:128, qb * 512 + off:(qb + 1) * 512]:
                           e.matmul(dst, l, r, start=True, stop=True, tile_position=(64, 0)),
                           inc="pe")
                wait("scalar", "pe", t_s)
                wait("scalar", "pe", pv_tok.get(tj - 3))
                src2 = s_ps[par][:, :].rearrange("p (h q) -> p h q", h=2)[:, :, off:512]
                dst2 = ptil[par3][:, :].rearrange("p (h q) -> p h q", h=2)[:, :, off:512]
                t_e = emit("scalar", lambda e, dst=dst2, src=src2:
                           e.activation(dst, src, AF.Exp), inc="act")
                sexp_tok[tj] = t_e
                exp_tok[tj] = ("act", t_e)
                if r_ >= 0:
                    wait("vector", "act", t_e)
                    wait("vector", "dc", tok_dc_all)
                    emit("vector", lambda e, a=ptil[par3][:, off:off + 128]:
                         e.tensor_mul(a, a, mtri[:, :]), inc="dve")
                    t_m = emit("vector", lambda e, a=ptil[par3][:, 512 + off:512 + off + 128]:
                               e.tensor_mul(a, a, mtri[:, :]), inc="dve")
                    exp_tok[tj] = ("dve", t_m)
                return t_s

            def emit_PV(j):
                tj = tk0 + j
                par3 = tj % 3
                et = exp_tok[tj]
                wait("tensor", et[0], et[1])
                if j == 0:
                    bwait("tensor", "pvbanks")
                    if pv_act_free[0] is not None:
                        wait("tensor", "act", pv_act_free[0])
                r_ = j - 4 * qb
                off = 128 * r_ if r_ >= 0 else 0
                emit("tensor", lambda e, dst=pvA[0:65, off:512],
                     l=Vst[:, j * 1040 + 65 * (2 * p): j * 1040 + 65 * (2 * p) + 65],
                     r=ptil[par3][:, off:512], j=j:
                     e.matmul(dst, l, r, start=(j == 0), stop=(j == nk - 1)))
                t_pv = emit("tensor", lambda e, dst=pvB[0:65, off:512],
                            l=Vst[:, j * 1040 + 65 * (2 * p + 1): j * 1040 + 65 * (2 * p + 1) + 65],
                            r=ptil[par3][:, 512 + off:1024], j=j:
                            e.matmul(dst, l, r, start=(j == 0), stop=(j == nk - 1)),
                            inc="pe")
                pv_tok[tj] = t_pv
                return t_pv

            t_s_last = emit_S(0)
            t_s_last = emit_S(1)
            for j in range(2, nk):
                t_s_last = emit_S(j)
                emit_PV(j - 2)
            do_lazy_slot()
            emit_PV(nk - 2)
            t_pv_last = emit_PV(nk - 1)
            last_s_tok[(p, qb)] = t_s_last
            gt[0] += nk

            hcol = (qb % QBH) * 512
            wait("vector", "pe", t_pv_last)
            bwait("vector", "stg")
            bwait("vector", "rzbuf")
            emit("vector", lambda e, d=zraw[64:65, 0:512]: e.tensor_copy(d, pvA[64:65, :]), inc="dve")
            emit("vector", lambda e, d=zraw[64:65, 512:1024]: e.tensor_copy(d, pvB[64:65, :]), inc="dve")
            emit("vector", lambda e, d=stgA[0:64, hcol:hcol + 512]:
                 e.tensor_copy(d, pvA[0:64, :]), inc="dve")
            t_u = emit("vector", lambda e, d=stgB[0:64, hcol:hcol + 512]:
                       e.tensor_copy(d, pvB[0:64, :]), inc="dve")
            sch.free_tok["pvbanks"] = ("dve", t_u)
            pv_act_free[0] = None
            wait("sync", "dve", t_u)
            t_dz = dma("sync", "dsz",
                       sc_z[p, :, qb * 512:(qb + 1) * 512],
                       zraw[64:65, :].rearrange("q (h t) -> q h t", h=2))
            bfree("rzbuf", "dsz", t_dz)
            if p + 1 < NPAIR and qb < NQB - 1:
                lazy_q.append(("proj", qb))

            if qb % QBH == QBH - 1 or qb == NQB - 1:
                while lazy_q:
                    do_lazy_slot()
                half = qb // QBH
                wait("sync", "dve", t_u)
                t1 = dma("sync", "dsg", sc_attn[p, 0:64, half * SHALF:(half + 1) * SHALF], stgA[:, :])
                t2 = dma("sync", "dsg", sc_attn[p, 64:128, half * SHALF:(half + 1) * SHALF], stgB[:, :])
                bfree("stg", "dsg", t2)
        if p + 1 < NPAIR:
            emit_proj_group(p + 1, NQB - 1, p)
        bfree(f"KTbuf{pw}", "pe", last_s_tok[(p, NQB - 1)])
        run_bg_conv(17)

    run_bg_conv(200)
    tok_wo_cv = bg_done["wo"]
    tok_w1_cv = bg_done["w1"]
    tok_w2_cv = bg_done["w2"]

    sch.barrier()
    ctxP2.close()
    ctxA.close()
    ctxB = ExitStack()

    def sbB(name, shape, dt):
        return ctxB.enter_context(nc.sbuf_tensor(name, shape, dt))

    wob = sbB("wob", [128, NCH * D], BF16)
    attn_in = [sbB(f"attn_in{i}", [128, NCH * 512], BF16) for i in range(2)]
    attn_s = [sbB(f"attn_s{i}", [128, NCH * 512], BF16) for i in range(2)]
    zrb = [sbB(f"zrb{i}", [128, NCH * 512], BF16) for i in range(2)]
    xTs = [sbB(f"xTs{i}", [128, NCH * 512], F32) for i in range(2)]
    x1st = [sbB(f"x1st{i}", [128, NCH * 512], F32) for i in range(2)]
    ZF = (NPAIR * 2 * S) // 128
    zq_f = sbB("zq_f", [128, ZF], F32)
    zq_b = sbB("zq_b", [128, ZF], BF16)

    ctxP3 = ExitStack()
    po = [ctxP3.enter_context(nc.psum_tensor(f"po{i}", [128, 512], F32)) for i in range(8)]

    wait("sync", "gp", tok_wo_cv)
    t_wob = dma("sync", "dwo", wob[:, :].rearrange("p (c i) -> p c i", c=NCH),
                sc_wob.rearrange("c p i -> p c i"))

    p3ld = {}

    t_zl = dma("scalar", "dzq", zq_f[:, :], bass.AP(sc_z.tensor, 0, [[ZF, 128], [1, ZF]]))
    wait("vector", "dzq", t_zl)
    t_zr = emit("vector", lambda e: e.reciprocal(zq_f[:, :], zq_f[:, :]), inc="dve")
    wait("vector", "dve", t_zr)
    t_zc = emit("vector", lambda e: e.tensor_copy(zq_b[:, :], zq_f[:, :]), inc="dve")
    wait("scalar", "dve", t_zc)
    t_zs = dma("scalar", "dzq", bass.AP(sc_z2.tensor, 0, [[ZF, 128], [1, ZF]]), zq_b[:, :])

    def p3_loads(tb):
        par = tb % 2
        bwait("sync", f"attn_in{par}")
        t_a = dma("sync", f"dai{par}", attn_in[par][:, :].rearrange("p (c t) -> p c t", c=NCH),
                  sc_attn[:, :, tb * 512:(tb + 1) * 512].rearrange("c p t -> p c t"))
        bwait("scalar", f"zrb{par}")
        wait("scalar", "dzq", t_zs)
        for hh in range(2):
            t_z = dma("scalar", f"dzr{par}",
                      zrb[par][64 * hh:64 * hh + 64, :].rearrange("p (c t) -> p c t", c=NCH),
                      bass.AP(sc_z2.tensor, hh * S + tb * 512, [[0, 64], [2 * S, NPAIR], [1, 512]]))
        bwait("gpsimd", f"xTs{par}")
        t_xr = dma("gpsimd", f"dxr{par}", xTs[par][:, :].rearrange("p (c t) -> p c t", c=NCH),
                   xT[:, tb * 512:(tb + 1) * 512].rearrange("(c p) t -> p c t", p=128))
        p3ld[tb] = (t_a, t_z, t_xr)

    p3_loads(0)
    for tb in range(NTB):
        par = tb % 2
        if tb + 1 < NTB:
            p3_loads(tb + 1)
        t_a, t_z, t_xr = p3ld[tb]
        wait("gpsimd", f"dai{par}", t_a)
        wait("gpsimd", f"dzr{par}", t_z)
        bwait("gpsimd", f"attn_s{par}")
        t_sc = emit("gpsimd", lambda e, d=attn_s[par][:, :], a=attn_in[par][:, :],
                    z=zrb[par][:, :]: e.tensor_mul(d, a, z), inc="gp")
        bfree(f"attn_in{par}", "gp", t_sc)
        bfree(f"zrb{par}", "gp", t_sc)
        wait("tensor", "gp", t_sc)
        if tb == 0:
            wait("tensor", "dwo", t_wob)
        mt_toks = []
        for mt in range(8):
            bwait("tensor", f"po{mt}")
            t_mm = None
            for c in range(NCH):
                t_mm = emit("tensor", lambda e, dst=po[mt][:, :],
                            l=wob[:, c * D + mt * 128: c * D + mt * 128 + 128],
                            r=attn_s[par][:, c * 512:(c + 1) * 512], c=c:
                            e.matmul(dst, l, r, start=(c == 0), stop=(c == NCH - 1)),
                            inc=("pe" if c == NCH - 1 else None))
            mt_toks.append(t_mm)
        wait("vector", f"dxr{par}", t_xr)
        wait("vector", "dc", tok_dc_all)
        t_x1 = None
        for mt in range(8):
            wait("vector", "pe", mt_toks[mt])
            t_x1 = emit("vector", lambda e, d=x1st[par][:, mt * 512:(mt + 1) * 512],
                        s=po[mt][:, :], b=bo_sb[:, mt:mt + 1],
                        x=xTs[par][:, mt * 512:(mt + 1) * 512]:
                        e.scalar_tensor_tensor(d, s, b, x, ALU.add, ALU.add), inc="dve")
            bfree(f"po{mt}", "dve", t_x1)
        bfree(f"attn_s{par}", "pe", mt_toks[-1])
        bfree(f"xTs{par}", "dve", t_x1)
        wait("gpsimd", "dve", t_x1)
        t_sx1 = dma("gpsimd", f"dsx{par}", sc_x1[:, :, tb * 512:(tb + 1) * 512].rearrange("c p t -> p c t"),
                    x1st[par][:, :].rearrange("p (c t) -> p c t", c=NCH))
        bfree(f"x1st{par}", f"dsx{par}", t_sx1)

    sch.barrier()
    ctxP3.close()
    ctxB.close()
    ctxC = ExitStack()

    def sbC(name, shape, dt):
        return ctxC.enter_context(nc.sbuf_tensor(name, shape, dt))

    x1s = [sbC(f"x1s{i}", [128, NCH * 512], F32) for i in range(2)]
    xn2s = [sbC(f"xn2s{i}", [128, NCH * 512], BF16) for i in range(2)]
    aT = sbC("aT", [128, NJ * 512], BF16)
    w1buf = [sbC(f"w1buf{i}", [128, W_FF], BF16) for i in range(2)]
    w2buf = [sbC(f"w2buf{i}", [128, W_FF // 2], BF16) for i in range(3)]
    cwb = sbC("cwb", [128, E * 512], BF16)
    rbuf = [sbC(f"rbuf{i}", [128, 512], BF16) for i in range(2)]
    tbuf = [sbC(f"tbuf{i}", [128, 512], BF16) for i in range(2)]
    ostg = [sbC(f"ostg{i}", [128, 512], F32) for i in range(2)]

    t1sC = sbC("t1sC", [128, 512], F32)
    prb = sbC("prb", [8, 512], F32)
    pn = sbC("pn", [128, 32], F32)
    tmp8 = sbC("tmp8", [128, 32], F32)
    m1c = sbC("m1c", [128, 1], F32)
    m2c = sbC("m2c", [128, 1], F32)
    zc = sbC("zc", [128, 1], F32)
    cwf = sbC("cwf", [128, 32], F32)
    cwT = sbC("cwT", [8, 512], BF16)

    ctxP4 = ExitStack()
    ST = ctxP4.enter_context(nc.psum_tensor("ST", [8, 1024], F32))
    PA = ctxP4.enter_context(nc.psum_tensor("PA", [128, 1024], F32))
    PB = ctxP4.enter_context(nc.psum_tensor("PB", [128, 1024], F32))
    YY = ctxP4.enter_context(nc.psum_tensor("YY", [128, 1024], F32))

    wait("sync", "gp", tok_w1_cv)
    wait("sync", "gp", tok_w2_cv)

    preps = {}
    prep_pending = [None]

    def make_prep(sl):
        par = sl % 2
        xs_img = x1s[par][:, :].rearrange("p (c t) -> p c t", c=NCH)
        st = {}

        def stage_load():
            bwait("sync", f"x1s{par}")
            st["t_x1"] = dma("sync", f"dx1_{par}", x1s[par][:, :].rearrange("p (c t) -> p c t", c=NCH),
                             sc_x1[:, :, sl * 512:(sl + 1) * 512].rearrange("c p t -> p c t"))

        xbw = xn2s[par][:, :]
        sqw = cwb[:, :]

        def stage_xb():
            wait("vector", f"dx1_{par}", st["t_x1"])
            bwait("vector", f"xn2s{par}")
            bwait("vector", "cwbK")
            t = None
            for c in range(NCH):
                t = emit("vector", lambda e, dst=xbw[:, c * 512:(c + 1) * 512], src=xs_img[:, c, :]:
                         e.tensor_copy(dst, src), inc="dve")
                wait("vector", "dve", t)
                t = emit("vector", lambda e, dst=sqw[:, c * 512:(c + 1) * 512],
                         src=xbw[:, c * 512:(c + 1) * 512]:
                         e.tensor_mul(dst, src, src), inc="dve")
                wait("vector", "dve", t)
            st["t_xb"] = t

        def stage_stats():
            wait("tensor", "dve", tok_ones)
            wait("tensor", "dve", st["t_xb"])
            t_mm = None
            bwait("tensor", "ST")
            for c in range(NCH):
                emit("tensor", lambda e, dst=ST[0:1, 0:512], src=xbw[:, c * 512:(c + 1) * 512], c=c:
                     e.matmul(dst, ones_b[:, :], src, start=(c == 0), stop=(c == NCH - 1)))
                t_mm = emit("tensor", lambda e, dst=ST[0:1, 512:1024], src=sqw[:, c * 512:(c + 1) * 512], c=c:
                            e.matmul(dst, ones_b[:, :], src, start=(c == 0), stop=(c == NCH - 1)),
                            inc="pe")
            bfree("cwbK2", "pe", t_mm)
            st["t_mm"] = t_mm
            wait("vector", "pe", t_mm)
            bwait("vector", "lnrows")
            t = emit("vector", lambda e: e.tensor_scalar_mul(m_row[:, :], ST[0:1, 0:512], 1.0 / D), inc="dve")
            wait("vector", "dve", t)
            t = emit("vector", lambda e: e.tensor_scalar_mul(v_row[:, :], ST[0:1, 512:1024], 1.0 / D), inc="dve")
            wait("vector", "dve", t)
            t = emit("vector", lambda e: e.tensor_mul(t_row[:, :], m_row[:, :], m_row[:, :]), inc="dve")
            wait("vector", "dve", t)
            t_var = emit("vector", lambda e: e.tensor_sub(v_row[:, :], v_row[:, :], t_row[:, :]), inc="dve")
            bfree("ST", "dve", t_var)
            wait("scalar", "dve", t_var)
            wait("scalar", "dve", tok_eps)
            t = emit("scalar", lambda e: e.activation(t_row[:, :], v_row[:, :], AF.Ln, bias=EPS), inc="act")
            wait("scalar", "act", t)
            st["t_r"] = emit("scalar", lambda e: e.activation(r_row[:, :], t_row[:, :], AF.Exp, scale=-0.5), inc="act")

        def stage_bcast():
            wait("tensor", "act", st["t_r"])
            bwait("tensor", "PBr")
            emit("tensor", lambda e: e.matmul(PB[:, 0:512], orow_f[:, :], m_row[:, :], start=True, stop=True))
            t_bc = emit("tensor", lambda e: e.matmul(PB[:, 512:1024], orow_f[:, :], r_row[:, :], start=True, stop=True), inc="pe")
            bfree("lnrows", "pe", t_bc)
            wait("vector", "pe", t_bc)
            bwait("vector", f"xn2s{par}")
            wait("vector", "pe", st["t_mm"])
            t_ap = None
            for c in range(NCH):
                t = emit("vector", lambda e, dst=t1sC[:, :], src=xs_img[:, c, :]:
                         e.tensor_sub(dst, src, PB[:, 0:512]), inc="dve")
                wait("vector", "dve", t)
                t_ap = emit("vector", lambda e, dst=xn2s[par][:, c * 512:(c + 1) * 512], src=t1sC[:, :]:
                            e.tensor_mul(dst, src, PB[:, 512:1024]), inc="dve")
                wait("vector", "dve", t_ap)
            st["t_ap"] = t_ap

        def stage_raw():
            wait("tensor", "dve", st["t_ap"])
            wait("tensor", "dve", tok_rwb)
            bwait("tensor", "ST")
            t_raw = None
            for c in range(NCH):
                t_raw = emit("tensor", lambda e, dst=ST[0:8, 0:512],
                             l=rwTb[:, c * 8:(c + 1) * 8],
                             r=xn2s[par][:, c * 512:(c + 1) * 512], c=c:
                             e.matmul(dst, l, r, start=(c == 0), stop=(c == NCH - 1)),
                             inc=("pe" if c == NCH - 1 else None))
            wait("scalar", "pe", t_raw)
            t_eN = emit("scalar", lambda e: e.activation(prb[:, :], ST[0:8, 0:512], AF.Exp, scale=-1.0), inc="act")
            sch.free_tok["ST"] = ("act", t_eN)
            st["t_probs"] = ("act", t_eN)

        def stage_ptp():
            wait("tensor", st["t_probs"][0], st["t_probs"][1])
            wait("tensor", "dc", tok_dc_all)
            t_tp = None
            for tt in range(4):
                t_tp = emit("tensor", lambda e, dst=PB[:, 512 + tt * 8: 512 + tt * 8 + 8],
                            src=prb[:, tt * 128:(tt + 1) * 128]:
                            e.transpose(dst, src, idf[0:8, 0:8]),
                            inc=("pe" if tt == 3 else None))
            wait("vector", "pe", t_tp)
            t = emit("vector", lambda e: e.tensor_copy(pn[:, :], PB[:, 512:544]), inc="dve")
            wait("vector", "dve", t)
            t = emit("vector", lambda e: e.tensor_scalar_add(pn[:, :], pn[:, :], 1.0), inc="dve")
            wait("vector", "dve", t)
            t = emit("vector", lambda e: e.reciprocal(pn[:, :], pn[:, :]), inc="dve")
            wait("vector", "dve", t)
            t_cw = None
            for tt in range(4):
                sl8 = slice(tt * 8, tt * 8 + 8)
                def ch(fn):
                    tt_ = emit("vector", fn, inc="dve")
                    wait("vector", "dve", tt_)
                    return tt_
                ch(lambda e, sl8=sl8: e.reduce_max(m1c[:, :], pn[:, sl8], axis=AX.X))
                ch(lambda e, sl8=sl8: e.tensor_scalar(tmp8[:, sl8], pn[:, sl8], m1c[:, :], None, ALU.is_ge))
                ch(lambda e, sl8=sl8: e.tensor_scalar_mul(tmp8[:, sl8], tmp8[:, sl8], -1e9))
                ch(lambda e, sl8=sl8: e.tensor_add(tmp8[:, sl8], tmp8[:, sl8], pn[:, sl8]))
                ch(lambda e, sl8=sl8: e.reduce_max(m2c[:, :], tmp8[:, sl8], axis=AX.X))
                ch(lambda e: e.tensor_add(zc[:, :], m1c[:, :], m2c[:, :]))
                ch(lambda e: e.tensor_scalar_add(zc[:, :], zc[:, :], 1e-20))
                ch(lambda e: e.reciprocal(zc[:, :], zc[:, :]))
                ch(lambda e, sl8=sl8: e.tensor_scalar(tmp8[:, sl8], pn[:, sl8], m2c[:, :], None, ALU.is_ge))
                ch(lambda e, sl8=sl8: e.tensor_mul(tmp8[:, sl8], tmp8[:, sl8], pn[:, sl8]))
                t_cw = ch(lambda e, sl8=sl8: e.tensor_scalar(cwf[:, sl8], tmp8[:, sl8], zc[:, :], None, ALU.mult))
            st["t_cw"] = t_cw

        def stage_cwb():
            wait("tensor", "dve", st["t_cw"])
            t_tb2 = None
            for tt in range(4):
                t_tb2 = emit("tensor", lambda e, dst=PB[0:8, 512 + tt * 128: 512 + (tt + 1) * 128],
                             src=cwf[:, tt * 8:(tt + 1) * 8]:
                             e.transpose(dst, src, idf[:, :]),
                             inc=("pe" if tt == 3 else None))
            wait("vector", "pe", t_tb2)
            t_cwT = emit("vector", lambda e: e.tensor_copy(cwT[:, :], PB[0:8, 512:1024]), inc="dve")
            bfree("PBr", "dve", t_cwT)
            wait("scalar", "dve", t_cwT)
            bwait("scalar", "dsccw")
            t_s1 = dma("scalar", "dcw", sc_cw[:, :], cwT[:, :])
            wait("scalar", "dcw", t_s1)
            bwait("scalar", "cwbK")
            bwait("scalar", "cwbK2")
            t_s2 = dma("scalar", "dcw",
                       cwb[:, :].rearrange("p (e t) -> p e t", e=E),
                       bass.AP(sc_cw.tensor, 0, [[0, 128], [512, E], [1, 512]]))
            sch.free_tok["dsccw"] = ("dcw", t_s2)
            preps[sl] = (par, t_s2)

        return [stage_load, stage_xb, stage_stats, stage_bcast, stage_raw, stage_ptp, stage_cwb]

    for f in make_prep(0):
        f()
    prep_pending[0] = None
    prev_r_last = None
    prev_a_last = None
    for sl in range(NSLAB):
        par, t_ready = preps[sl]

        r_tok = {-2: prev_r_last, -1: prev_r_last}
        a_tok = {-2: prev_a_last, -1: prev_a_last}
        h_last = None
        for J in range(NCH):
            parw = J % 2
            bwait("gpsimd", f"w1buf{parw}")
            t_w1 = dma("gpsimd", f"dw1_{parw}", w1buf[parw][:, :], sc_w1b[J, :, :])
            wait("tensor", f"dw1_{parw}", t_w1)
            for u in range(NCH):
                j = 8 * J + u
                parh = j % 2
                ee = j // 8
                wait("tensor", "act", r_tok.get(j - 2))
                t_h = None
                for c in range(NCH):
                    t_h = emit("tensor", lambda e, dst=PA[:, parh * 512:(parh + 1) * 512],
                               l=w1buf[parw][:, u * 1024 + c * 128: u * 1024 + c * 128 + 128],
                               r=xn2s[par][:, c * 512:(c + 1) * 512], c=c:
                               e.matmul(dst, l, r, start=(c == 0), stop=(c == NCH - 1)),
                               inc=("pe" if c == NCH - 1 else None))
                h_last = t_h
                wait("scalar", "pe", t_h)
                wait("scalar", "dve", a_tok.get(j - 2))
                wait("scalar", "act", r_tok.get(j - 2))
                t_r = emit("scalar", lambda e, dst=rbuf[parh][:, :],
                           src=PA[:, parh * 512:(parh + 1) * 512]:
                           e.activation(dst, src, AF.Relu), inc="act")
                r_tok[j] = t_r
                wait("vector", "act", t_r)
                bwait("vector", "aT")
                t = emit("vector", lambda e, d=tbuf[parh][:, :], s=rbuf[parh][:, :]:
                         e.tensor_mul(d, s, s), inc="dve")
                wait("vector", "dve", t)
                if j == 0:
                    wait("vector", "dcw", t_ready)
                t_a = emit("vector", lambda e, d=aT[:, j * 512:(j + 1) * 512],
                           s=tbuf[parh][:, :], w=cwb[:, ee * 512:(ee + 1) * 512]:
                           e.tensor_mul(d, s, w), inc="dve")
                a_tok[j] = t_a
            bfree(f"w1buf{parw}", "pe", h_last)
        bfree("PAbc", "act", r_tok[NJ - 1])
        bfree("cwbK", "dve", a_tok[NJ - 1])
        sch.free_tok["PBr2"] = ("dve", a_tok[NJ - 1])
        bfree(f"xn2s{par}", "pe", h_last)
        prev_r_last = r_tok[NJ - 1]
        prev_a_last = a_tok[NJ - 1]

        prep_stages = make_prep(sl + 1) if sl + 1 < NSLAB else []
        slot_map = {0: 0, 1: 1, 2: 2, 3: 3, 4: 4, 5: 5, 6: 6}
        y_last = None
        for mt in range(8):
            if mt in slot_map and slot_map[mt] < len(prep_stages):
                prep_stages[slot_map[mt]]()
            for hf in range(2):
                parw = (2 * mt + hf) % 3
                bwait("sync", f"w2buf{parw}")
                t_w2 = dma("sync", f"dw2_{parw}", w2buf[parw][:, :],
                           sc_w2b[mt, :, hf * 4096:(hf + 1) * 4096])
                wait("tensor", f"dw2_{parw}", t_w2)
                if hf == 0:
                    wait("tensor", "dve", a_tok[NJ // 2 - 1])
                    bwait("tensor", f"yy{mt % 2}")
                else:
                    wait("tensor", "dve", a_tok[NJ - 1])
                t_y = None
                for jj in range(NJ // 2):
                    jt = hf * 32 + jj
                    t_y = emit("tensor", lambda e, dst=YY[:, (mt % 2) * 512:(mt % 2 + 1) * 512],
                               l=w2buf[parw][:, jj * 128:(jj + 1) * 128],
                               r=aT[:, jt * 512:(jt + 1) * 512], jt=jt:
                               e.matmul(dst, l, r, start=(jt == 0), stop=(jt == NJ - 1)),
                               inc=("pe" if jj == NJ // 2 - 1 else None))
                bfree(f"w2buf{parw}", "pe", t_y)
            wait("vector", "pe", t_y)
            bwait("vector", f"ostg{mt % 2}")
            t_o = emit("vector", lambda e, d=ostg[mt % 2][:, :],
                       a=YY[:, (mt % 2) * 512:(mt % 2 + 1) * 512],
                       b=x1s[par][:, mt * 512:(mt + 1) * 512]:
                       e.tensor_add(d, a, b), inc="dve")
            bfree(f"yy{mt % 2}", "dve", t_o)
            y_last = t_o
            wait("scalar", "dve", t_o)
            t_out = dma("scalar", f"dos{mt % 2}", outT[mt * 128:(mt + 1) * 128, sl * 512:(sl + 1) * 512],
                        ostg[mt % 2][:, :])
            bfree(f"ostg{mt % 2}", f"dos{mt % 2}", t_out)
        bfree(f"x1s{par}", "dve", y_last)
        bfree("aT", "pe", sch.cnt["pe"])

    sch.barrier()

    for name in sorted(sch.sems):
        semh[name] = ctxO.enter_context(nc.semaphore(name))
    sch.play(nc, semh)
    ctxP4.close()
    ctxC.close()
    ctxO.close()
    return nc


_MASKTRI = np.triu(np.ones((128, 128), np.float32)).astype(bfloat16)
_IDENT = np.eye(128, dtype=np.float32)


def kernel(L0_attn_x, L0_attn_Wq, L0_attn_Wk, L0_attn_Wv, L0_attn_Wo, L0_attn_bo,
           L0_ffn_router_w, L0_ffn_w1, L0_ffn_w2):
    x = np.asarray(L0_attn_x, dtype=np.float32)
    B, S, _ = x.shape
    nc = build(S)
    shared = {
        "wq": np.ascontiguousarray(np.asarray(L0_attn_Wq, np.float32)),
        "wk": np.ascontiguousarray(np.asarray(L0_attn_Wk, np.float32)),
        "wv": np.ascontiguousarray(np.asarray(L0_attn_Wv, np.float32)),
        "wo": np.ascontiguousarray(np.asarray(L0_attn_Wo, np.float32)),
        "bo": np.ascontiguousarray(np.asarray(L0_attn_bo, np.float32).reshape(NCH, 128).T),
        "rwT": np.ascontiguousarray(np.asarray(L0_ffn_router_w, np.float32).T),
        "w1": np.ascontiguousarray(np.asarray(L0_ffn_w1, np.float32)),
        "w2": np.ascontiguousarray(np.asarray(L0_ffn_w2, np.float32)),
        "identf": _IDENT,
        "masktri": _MASKTRI,
    }
    in_maps = []
    for c in range(B):
        m = dict(shared)
        m["xT"] = np.ascontiguousarray(x[c].T)
        in_maps.append(m)
    res = run_bass_kernel_spmd(nc, in_maps, core_ids=list(range(B)))
    out = np.stack([np.asarray(res.results[c]["outT"]).T for c in range(B)], axis=0)
    return out.astype(np.float32)


if __name__ == "__main__":
    nc = build(int(sys.argv[1]) if len(sys.argv) > 1 else 512)
    print("build ok")
